# version 15
# baseline (speedup 1.0000x reference)
import numpy as np
import ml_dtypes
import concourse.bass as bass
import concourse.mybir as mybir
from concourse.bass_utils import run_bass_kernel_spmd

F32 = mybir.dt.float32
BF16 = mybir.dt.bfloat16
AF = mybir.ActivationFunctionType
ALU = mybir.AluOpType
AX = mybir.AxisListType
BF = ml_dtypes.bfloat16

S, D, H, HD, W = 8192, 2048, 16, 128, 8
T = S // W
HPC = H // W
NK = D // 128
TC = T // 128
NQB = S // 512
EPS = 1e-5
SM_SCALE = float(1.0 / np.sqrt(HD))
SHIFT = -20.0

_uid = [0]


def _u(name):
    _uid[0] += 1
    return f"{name}_{_uid[0]}"


class Sem:
    def __init__(self, nc, name):
        self.h = nc.alloc_semaphore(_u(name))
        self.v = 0

    def inc(self, instr, k=1):
        instr.then_inc(self.h, k)
        self.v += k
        return self.v

    def wait(self, eng, v=None):
        eng.wait_ge(self.h, self.v if v is None else v)


class DSem:

    def __init__(self, nc, name):
        self.s = [Sem(nc, name + "a"), Sem(nc, name + "b")]

    def inc(self, i, instr):
        return self.s[i % 2].inc(instr, 16)

    def wait_done(self, eng, i):
        if i < 0:
            return
        self.s[i % 2].wait(eng, 16 * (i // 2 + 1))

    def wait_all(self, eng):
        self.s[0].wait(eng)
        self.s[1].wait(eng)


class DSem4:

    def __init__(self, nc, name):
        self.s = [Sem(nc, name + str(i)) for i in range(4)]

    def inc(self, i, instr):
        return self.s[i % 4].inc(instr, 16)

    def wait_done(self, eng, i):
        if i < 0:
            return
        self.s[i % 4].wait(eng, 16 * (i // 4 + 1))

    def wait_all(self, eng):
        for s in self.s:
            s.wait(eng)


class Consts:
    def __init__(self, nc):
        self.ones_col_f = nc.alloc_sbuf_tensor(_u("c_ocf"), [128, 1], F32)
        self.ones_col_b = nc.alloc_sbuf_tensor(_u("c_ocb"), [128, 1], BF16)
        self.ones_row_f = nc.alloc_sbuf_tensor(_u("c_orf"), [1, 128], F32)
        self.one1_f = nc.alloc_sbuf_tensor(_u("c_o1f"), [1, 1], F32)
        self.bias_sh = nc.alloc_sbuf_tensor(_u("c_bsh"), [128, 1], F32)
        self.eps_t = nc.alloc_sbuf_tensor(_u("c_eps"), [1, 1], F32)

    def emit(self, nc):
        with nc.Block(_u("consts")) as blk:
            @blk.vector
            def _(e):
                e.memset(self.ones_col_f[:, :], 1.0)
                e.memset(self.ones_col_b[:, :], 1.0)
                e.memset(self.ones_row_f[:, :], 1.0)
                e.memset(self.one1_f[:, :], 1.0)
                e.memset(self.bias_sh[:, :], SHIFT)
                e.memset(self.eps_t[:, :], EPS)


def emit_loads(nc, loads, name="inputs"):
    s = Sem(nc, "s_in")
    with nc.Block(name) as blk:
        @blk.sync
        def _(e):
            for out_ap, in_ap in loads:
                s.inc(e.dma_start(out=out_ap, in_=in_ap), 16)
            s.wait(e)



def emit_ln(nc, C, xT, xlnT, mrcol, gate=None):
    import contextlib
    with (
        nc.sbuf_tensor(_u("mB"), [128, 1024], F32) as mB,
        nc.sbuf_tensor(_u("rB"), [128, 1024], F32) as rB,
        nc.sbuf_tensor(_u("tmp_n"), [128, (1 if gate else 2) * 1024],
                       F32) as tmp_n,
    ):
        _ps = contextlib.ExitStack()
        sq_sc = _ps.enter_context(
            nc.sbuf_tensor(_u("sq_sc"), [128, 2 * 1024], BF16))
        xb_sc = _ps.enter_context(
            nc.sbuf_tensor(_u("xb_sc"), [128, 2 * 1024], BF16))
        rows = _ps.enter_context(
            nc.sbuf_tensor(_u("rows"), [1, 3 * 1024], F32))
        p_sx = _ps.enter_context(nc.psum_tensor(_u("p_sx"), [1, 1024], F32))
        p_sq = _ps.enter_context(nc.psum_tensor(_u("p_sq"), [1, 1024], F32))
        p_b0 = _ps.enter_context(nc.psum_tensor(_u("p_b0"), [128, 512], F32))
        p_b1 = _ps.enter_context(nc.psum_tensor(_u("p_b1"), [128, 512], F32))
        p_mr = _ps.enter_context(nc.psum_tensor(_u("p_mr"), [128, 16], F32))
        s_sq = Sem(nc, "ln_sq")
        s_pe = Sem(nc, "ln_pe")
        s_ve = Sem(nc, "ln_ve")
        s_se = Sem(nc, "ln_se")

        with nc.Block(_u("ln_stats")) as blk:
            @blk.scalar
            def _(e):
                for k in range(NK):
                    if k >= 2:
                        s_pe.wait(e, 2 * (k - 1))
                    e.activation(
                        out=xb_sc[:, (k % 2) * 1024:(k % 2) * 1024 + 1024],
                        in_=xT[:, k * T:k * T + T], func=AF.Copy)
                    s_sq.inc(e.activation(
                        out=sq_sc[:, (k % 2) * 1024:(k % 2) * 1024 + 1024],
                        in_=xT[:, k * T:k * T + T], func=AF.Square))

            @blk.tensor
            def _(e):
                for k in range(NK):
                    st, sp = (k == 0), (k == NK - 1)
                    s_sq.wait(e, k + 1)
                    e.matmul(p_sx[0:1, 0:512], C.ones_col_b[:, :],
                             xb_sc[:, (k % 2) * 1024:(k % 2) * 1024 + 512],
                             start=st, stop=sp)
                    e.matmul(p_sx[0:1, 512:1024], C.ones_col_b[:, :],
                             xb_sc[:, (k % 2) * 1024 + 512:(k % 2) * 1024 + 1024],
                             start=st, stop=sp)
                    e.matmul(p_sq[0:1, 0:512], C.ones_col_b[:, :],
                             sq_sc[:, (k % 2) * 1024:(k % 2) * 1024 + 512],
                             start=st, stop=sp)
                    ins = e.matmul(
                        p_sq[0:1, 512:1024], C.ones_col_b[:, :],
                        sq_sc[:, (k % 2) * 1024 + 512:(k % 2) * 1024 + 1024],
                        start=st, stop=sp)
                    s_pe.inc(ins, 2)

        m_row = rows[0:1, 0:1024]
        r_row = rows[0:1, 1024:2048]
        ex2 = rows[0:1, 2048:3072]
        m2 = rows[0:1, 1024:2048]
        var = rows[0:1, 2048:3072]
        sd = rows[0:1, 1024:2048]
        with nc.Block(_u("ln_fin")) as blk:
            @blk.vector
            def _(e):
                s_ve.inc(e.tensor_scalar_mul(m_row, p_sx[0:1, :], 1.0 / D))
                s_ve.inc(e.tensor_scalar_mul(ex2, p_sq[0:1, :], 1.0 / D))
                s_ve.wait(e)
                s_ve.inc(e.tensor_mul(m2, m_row, m_row))
                s_ve.wait(e)
                s_ve.inc(e.tensor_sub(var, ex2, m2))

            @blk.scalar
            def _(e):
                s_ve.wait(e)
                s_se.inc(e.activation(out=sd, in_=var, func=AF.Sqrt,
                                      bias=C.eps_t[0:1, 0:1]))

            @blk.vector
            def _(e):
                s_se.wait(e)
                s_ve.inc(e.reciprocal(r_row, sd))

        with nc.Block(_u("ln_bc")) as blk:
            @blk.tensor
            def _(e):
                for tc in range(TC):
                    e.matmul(p_mr[:, tc:tc + 1],
                             m_row[0:1, 128 * tc:128 * tc + 128],
                             C.one1_f[0:1, 0:1], start=True, stop=True)
                    e.matmul(p_mr[:, TC + tc:TC + tc + 1],
                             r_row[0:1, 128 * tc:128 * tc + 128],
                             C.one1_f[0:1, 0:1], start=True, stop=True)
                e.matmul(p_b0[:, :], C.ones_row_f[0:1, :], m_row[0:1, 0:512],
                         start=True, stop=True)
                s_pe.inc(e.matmul(p_b1[:, :], C.ones_row_f[0:1, :],
                                  m_row[0:1, 512:1024], start=True, stop=True))

            @blk.vector
            def _(e):
                s_pe.wait(e)
                s_ve.inc(e.tensor_copy(out=mrcol[:, :], in_=p_mr[:, :]))
                s_ve.inc(e.tensor_copy(out=mB[:, 0:512], in_=p_b0[:, :]))
                s_ve.inc(e.tensor_copy(out=mB[:, 512:1024], in_=p_b1[:, :]))

            @blk.tensor
            def _(e):
                s_ve.wait(e)
                e.matmul(p_b0[:, :], C.ones_row_f[0:1, :], r_row[0:1, 0:512],
                         start=True, stop=True)
                s_pe.inc(e.matmul(p_b1[:, :], C.ones_row_f[0:1, :],
                                  r_row[0:1, 512:1024], start=True, stop=True))

            @blk.vector
            def _(e):
                s_pe.wait(e)
                s_ve.inc(e.tensor_copy(out=rB[:, 0:512], in_=p_b0[:, :]))
                s_ve.inc(e.tensor_copy(out=rB[:, 512:1024], in_=p_b1[:, :]))

        _ps.close()

        def norm_ops(e, ks, sem, half):
            for k in ks:
                sem.inc(e.tensor_sub(tmp_n[:, half * 1024:half * 1024 + 1024],
                                     xT[:, k * T:(k + 1) * T], mB[:, :]))
                sem.wait(e)
                sem.inc(e.tensor_mul(xlnT[:, k * T:(k + 1) * T],
                                     tmp_n[:, half * 1024:half * 1024 + 1024],
                                     rB[:, :]))
                sem.wait(e)

        if gate is None:
            with nc.Block(_u("ln_norm")) as blk:
                s_n = Sem(nc, "ln_n")
                s_n2 = Sem(nc, "ln_n2")

                @blk.vector
                def _(e):
                    norm_ops(e, range(0, 10), s_n, 0)

                @blk.gpsimd
                def _(e):
                    norm_ops(e, range(10, NK), s_n2, 1)
            return

        wg_sb, ind, indtv = gate["wg_sb"], gate["ind"], gate["indtv"]
        with (
            nc.psum_tensor(_u("p_ws"), [1, 16], F32) as p_ws,
            nc.psum_tensor(_u("p_wb"), [128, 16], F32) as p_wb,
            nc.psum_tensor(_u("p_gl"), [128, 8 * 16], F32) as p_gl,
            nc.sbuf_tensor(_u("g_sc"), [128, 512], F32) as g_sc,
            nc.sbuf_tensor(_u("ws_row"), [1, 16], F32) as ws_row,
            nc.sbuf_tensor(_u("wsB"), [128, 16], F32) as wsB,
        ):
            raw = g_sc[:, 0:128]
            lgs = g_sc[:, 128:256]
            nlmax = g_sc[:, 256:264]
            exs = g_sc[:, 272:400]
            ssum = g_sc[:, 400:408]
            tvc = g_sc[:, 408:416]
            t2 = g_sc[:, 432:448]
            nmcol = g_sc[:, 448:456]
            s_g1 = Sem(nc, "g1")
            s_g2 = Sem(nc, "g2")
            s_ga = Sem(nc, "ga")
            s_gb = Sem(nc, "gb")
            s_gc = Sem(nc, "gc")
            s_n2 = Sem(nc, "ln_n2")
            with nc.Block(_u("norm_gate")) as blk:
                @blk.gpsimd
                def _(e):
                    norm_ops(e, range(NK), s_n2, 0)

                @blk.tensor
                def _(e):
                    ins = None
                    for k in range(NK):
                        ins = e.matmul(p_ws[0:1, :], C.ones_col_f[:, :],
                                       wg_sb[:, 16 * k:16 * k + 16],
                                       start=(k == 0), stop=(k == NK - 1))
                    s_g1.inc(ins)
                    ins = None
                    for tc in range(TC):
                        for k in range(NK):
                            ins = e.matmul(
                                p_gl[:, 16 * tc:16 * tc + 16],
                                xT[:, k * T + 128 * tc:k * T + 128 * tc + 128],
                                wg_sb[:, 16 * k:16 * k + 16],
                                start=(k == 0), stop=(k == NK - 1))
                    s_ga.inc(ins)

                @blk.vector
                def _(e):
                    s_g1.wait(e, 1)
                    s_g2.inc(e.tensor_copy(out=ws_row[:, :], in_=p_ws[0:1, :]))

                @blk.tensor
                def _(e):
                    s_g2.wait(e, 1)
                    s_g1.inc(e.matmul(p_wb[:, :], C.ones_row_f[0:1, :],
                                      ws_row[0:1, :], start=True, stop=True))

                @blk.vector
                def _(e):
                    s_g1.wait(e, 2)
                    s_g2.inc(e.tensor_copy(out=wsB[:, :], in_=p_wb[:, :]))
                    s_g2.inc(e.tensor_scalar_mul(nmcol, mrcol[:, 0:TC], -1.0))
                    s_g2.wait(e)
                    s_ga.wait(e)
                    for tc in range(TC):
                        sl = slice(16 * tc, 16 * tc + 16)
                        s_gb.inc(e.scalar_tensor_tensor(
                            t2, wsB[:, :], nmcol[:, tc:tc + 1], p_gl[:, sl],
                            ALU.mult, ALU.add))
                        s_gb.wait(e)
                        s_gb.inc(e.tensor_scalar(lgs[:, sl], t2,
                                                 mrcol[:, TC + tc:TC + tc + 1],
                                                 None, ALU.mult))
                        s_gb.wait(e)
                        s_gb.inc(e.reduce_max(nlmax[:, tc:tc + 1], lgs[:, sl],
                                              axis=AX.X, negate=True))
                        s_gb.wait(e)
                        s_gb.inc(e.tensor_single_scalar(
                            ind[:, sl], lgs[:, sl], nlmax[:, tc:tc + 1],
                            ALU.add))
                        s_gb.wait(e)
                        s_gb.inc(e.tensor_single_scalar(
                            ind[:, sl], ind[:, sl], 0.0, ALU.is_ge))
                        s_gb.wait(e)

                @blk.scalar
                def _(e):
                    s_gb.wait(e)
                    ins = None
                    for tc in range(TC):
                        sl = slice(16 * tc, 16 * tc + 16)
                        ins = e.activation(out=exs[:, sl], in_=lgs[:, sl],
                                           func=AF.Exp,
                                           bias=nlmax[:, tc:tc + 1])
                    s_gc.inc(ins)

                @blk.vector
                def _(e):
                    s_gc.wait(e)
                    for tc in range(TC):
                        sl = slice(16 * tc, 16 * tc + 16)
                        s_gb.inc(e.reduce_sum(ssum[:, tc:tc + 1], exs[:, sl],
                                              axis=AX.X))
                        s_gb.wait(e)
                        s_gb.inc(e.reciprocal(tvc[:, tc:tc + 1],
                                              ssum[:, tc:tc + 1]))
                        s_gb.wait(e)
                        s_gb.inc(e.tensor_scalar(indtv[:, sl], ind[:, sl],
                                                 tvc[:, tc:tc + 1], None,
                                                 ALU.mult))
                        s_gb.wait(e)



def emit_proj_feat(nc, C, w_dram, n_oc, srcT, out_dram, out_f32_add=None,
                   out_dram_f32=None):
    wbf = w_dram.dtype == BF16
    with (
        nc.psum_tensor(_u("pp0"), [128, 1024], F32) as pp0,
        nc.psum_tensor(_u("pp1"), [128, 1024], F32) as pp1,
        nc.sbuf_tensor(_u("wf"), [128, 2 * NK * 128],
                       BF16 if wbf else F32) as wf,
        nc.sbuf_tensor(_u("wb"), [128, 3 * NK * 128], BF16) as wb,
        nc.sbuf_tensor(_u("stg"), [128, 2 * 1024],
                       BF16 if out_f32_add is None else F32) as stg,
    ):
        if wbf:
            wb = wf
        s_wd = DSem(nc, "pf_wd")
        s_wc = Sem(nc, "pf_wc")
        s_pe = Sem(nc, "pf_pe")
        s_ev = Sem(nc, "pf_ev")
        s_st = DSem(nc, "pf_st")
        s_wb = Sem(nc, "pf_wb")
        pp = [pp0, pp1]
        with nc.Block(_u("proj_feat")) as blk:
            @blk.sync
            def _(e):
                for oc in range(n_oc):
                    par = oc % 2
                    if oc >= 2:
                        if wbf:
                            s_pe.wait(e, oc - 1)
                        else:
                            s_wc.wait(e, oc - 1)
                    s_wd.inc(oc, e.dma_start(
                        out=wf[:, par * NK * 128:(par + 1) * NK * 128]
                        .rearrange("p (k o) -> p k o", o=128),
                        in_=w_dram[:, 128 * oc:128 * oc + 128]
                        .rearrange("(k p) o -> p k o", p=128)))

            @blk.gpsimd
            def _(e):
                if wbf:
                    return
                for oc in range(n_oc):
                    par = oc % 2
                    par3 = oc % 3
                    s_wd.wait_done(e, oc)
                    if oc >= 3:
                        s_pe.wait(e, oc - 2)
                    s_wc.inc(e.tensor_copy(
                        out=wb[:, par3 * NK * 128:(par3 + 1) * NK * 128],
                        in_=wf[:, par * NK * 128:(par + 1) * NK * 128]))

            @blk.tensor
            def _(e):
                for oc in range(n_oc):
                    par = oc % 2
                    if wbf:
                        s_wd.wait_done(e, oc)
                        wbase = par * NK * 128
                    else:
                        s_wc.wait(e, oc + 1)
                        wbase = (oc % 3) * NK * 128
                    if oc >= 2:
                        s_ev.wait(e, oc - 1)
                    ins = None
                    for k in range(NK):
                        st, sp = (k == 0), (k == NK - 1)
                        base = wbase + 128 * k
                        e.matmul(pp[par][:, 0:512], wb[:, base:base + 128],
                                 srcT[:, k * T:k * T + 512], start=st, stop=sp)
                        ins = e.matmul(pp[par][:, 512:1024],
                                       wb[:, base:base + 128],
                                       srcT[:, k * T + 512:k * T + 1024],
                                       start=st, stop=sp)
                    s_pe.inc(ins)

            @blk.vector
            def _(e):
                for oc in range(n_oc):
                    par = oc % 2
                    s_pe.wait(e, oc + 1)
                    if oc >= 2:
                        s_st.wait_done(e, oc - 2)
                        if out_f32_add is not None:
                            s_wb.wait(e, oc - 1)
                    if out_f32_add is None:
                        s_ev.inc(e.tensor_copy(
                            out=stg[:, par * 1024:(par + 1) * 1024],
                            in_=pp[par][:, :]))
                    else:
                        s_ev.inc(e.tensor_add(
                            stg[:, par * 1024:(par + 1) * 1024],
                            pp[par][:, :],
                            out_f32_add[:, oc * T:(oc + 1) * T]))

            @blk.scalar
            def _(e):
                dram = out_dram if out_dram_f32 is None else out_dram_f32
                for oc in range(n_oc):
                    par = oc % 2
                    s_ev.wait(e, oc + 1)
                    if out_f32_add is not None:
                        s_wb.inc(e.copy(
                            out=out_f32_add[:, oc * T:(oc + 1) * T],
                            in_=stg[:, par * 1024:(par + 1) * 1024]))
                    s_st.inc(oc, e.dma_start(
                        out=dram[128 * oc:128 * oc + 128, :],
                        in_=stg[:, par * 1024:(par + 1) * 1024]))
                s_st.wait_all(e)


def emit_proj_tok(nc, C, w_dram, n_cg, srcT, consume):
    wbf = w_dram.dtype == BF16
    with (
        nc.psum_tensor(_u("pt0"), [128, 512], F32) as b0,
        nc.psum_tensor(_u("pt1"), [128, 512], F32) as b1,
        nc.psum_tensor(_u("pt2"), [128, 512], F32) as b2,
        nc.psum_tensor(_u("pt3"), [128, 512], F32) as b3,
        nc.psum_tensor(_u("pt4"), [128, 512], F32) as b4,
        nc.psum_tensor(_u("pt5"), [128, 512], F32) as b5,
        nc.psum_tensor(_u("pt6"), [128, 512], F32) as b6,
        nc.psum_tensor(_u("pt7"), [128, 512], F32) as b7,
        nc.sbuf_tensor(_u("wtf"), [128, 2 * 8 * 512],
                       BF16 if wbf else F32) as wtf,
        nc.sbuf_tensor(_u("wtb"), [128, 4 * 512], BF16) as wtb,
    ):
        banks = [b0, b1, b2, b3, b4, b5, b6, b7]
        s_wd = DSem(nc, "pt_wd")
        s_wc = Sem(nc, "pt_wc")
        s_pe = Sem(nc, "pt_pe")
        s_ev = Sem(nc, "pt_ev")
        KG = NK // 8
        with nc.Block(_u("proj_tok")) as blk:
            @blk.sync
            def _(e):
                for cg in range(n_cg):
                    for kg in range(KG):
                        i2 = cg * KG + kg
                        par = i2 % 2
                        if i2 >= 2:
                            if wbf:
                                s_pe.wait(e, 8 * (i2 - 1))
                            else:
                                s_wc.wait(e, 8 * (i2 - 1))
                        s_wd.inc(i2, e.dma_start(
                            out=wtf[:, par * 8 * 512:(par + 1) * 8 * 512]
                            .rearrange("p (k c) -> p k c", c=512),
                            in_=w_dram[1024 * kg:1024 * kg + 1024,
                                       512 * cg:512 * cg + 512]
                            .rearrange("(k p) c -> p k c", p=128)))

            @blk.scalar
            def _(e):
                if wbf:
                    return
                for cg in range(n_cg):
                    for k in range(NK):
                        i = cg * NK + k
                        par = i % 4
                        i2 = cg * KG + k // 8
                        s_wd.wait_done(e, i2)
                        if i >= 4:
                            s_pe.wait(e, i - 3)
                        s_wc.inc(e.copy(
                            out=wtb[:, par * 512:(par + 1) * 512],
                            in_=wtf[:, (i2 % 2) * 8 * 512 + (k % 8) * 512:
                                    (i2 % 2) * 8 * 512 + (k % 8) * 512 + 512]))

            @blk.tensor
            def _(e):
                for cg in range(n_cg):
                    for k in range(NK):
                        i = cg * NK + k
                        par = i % 2
                        i2 = cg * KG + k // 8
                        if wbf:
                            s_wd.wait_done(e, i2)
                            wop = wtf[:, (i2 % 2) * 8 * 512 + (k % 8) * 512:
                                      (i2 % 2) * 8 * 512 + (k % 8) * 512 + 512]
                        else:
                            s_wc.wait(e, i + 1)
                            wop = wtb[:, (i % 4) * 512:(i % 4) * 512 + 512]
                        if k == 0 and cg >= 1:
                            s_ev.wait(e, 8 * cg)
                        ins = None
                        for tc in range(TC):
                            ins = e.matmul(
                                banks[tc][:, :],
                                srcT[:, k * T + 128 * tc:k * T + 128 * tc + 128],
                                wop,
                                start=(k == 0), stop=(k == NK - 1))
                        s_pe.inc(ins)

            @blk.vector
            def _(e):
                for cg in range(n_cg):
                    s_pe.wait(e, NK * (cg + 1))
                    consume(e, cg, banks, s_ev)




def proj_tok_res_alloc(nc):
    return nc.alloc_sbuf_tensor(_u("wres"), [128, 2 * NK * 1024], BF16)


def proj_tok_res_prefetch(nc, w_dram, wres, s_wd):
    n_pass = w_dram.shape[1] // 1024
    with nc.Block(_u("wpref")) as blk:
        @blk.sync
        def _(e):
            for p in range(min(2, n_pass)):
                s_wd.inc(p, e.dma_start(
                    out=wres[:, (p % 2) * NK * 1024:(p % 2 + 1) * NK * 1024]
                    .rearrange("p (k c) -> p k c", c=1024),
                    in_=w_dram[:, 1024 * p:1024 * p + 1024]
                    .rearrange("(k p) c -> p k c", p=128)))


def emit_proj_tok_res(nc, C, w_dram, srcT, consume_tc, wres, s_wd,
                      dma_out=None):
    n_pass = w_dram.shape[1] // 1024
    with (
        nc.psum_tensor(_u("pr0"), [128, 512], F32) as b0,
        nc.psum_tensor(_u("pr1"), [128, 512], F32) as b1,
        nc.psum_tensor(_u("pr2"), [128, 512], F32) as b2,
        nc.psum_tensor(_u("pr3"), [128, 512], F32) as b3,
        nc.psum_tensor(_u("pr4"), [128, 512], F32) as b4,
        nc.psum_tensor(_u("pr5"), [128, 512], F32) as b5,
        nc.psum_tensor(_u("pr6"), [128, 512], F32) as b6,
        nc.psum_tensor(_u("pr7"), [128, 512], F32) as b7,
    ):
        banks = [b0, b1, b2, b3, b4, b5, b6, b7]
        s_pe = Sem(nc, "pr_pe")
        s_ev = Sem(nc, "pr_ev")
        with nc.Block(_u("proj_res")) as blk:
            @blk.sync
            def _(e):
                for p in range(2, n_pass):
                    s_pe.wait(e, 8 * (p - 1))
                    s_wd.inc(p, e.dma_start(
                        out=wres[:, (p % 2) * NK * 1024:(p % 2 + 1) * NK * 1024]
                        .rearrange("p (k c) -> p k c", c=1024),
                        in_=w_dram[:, 1024 * p:1024 * p + 1024]
                        .rearrange("(k p) c -> p k c", p=128)))

            @blk.tensor
            def _(e):
                for p in range(n_pass):
                    par = p % 2
                    s_wd.wait_done(e, p)
                    for tc in range(TC):
                        idx = p * TC + tc
                        if idx >= 4:
                            s_ev.wait(e, 2 * (idx - 3))
                        bb = [banks[(tc % 4) * 2], banks[(tc % 4) * 2 + 1]]
                        ins = None
                        for k in range(NK):
                            wof = par * NK * 1024 + k * 1024
                            for cg2 in range(2):
                                ins = e.matmul(
                                    bb[cg2][:, :],
                                    srcT[:, k * T + 128 * tc:
                                         k * T + 128 * tc + 128],
                                    wres[:, wof + 512 * cg2:wof + 512 * cg2 + 512],
                                    start=(k == 0), stop=(k == NK - 1))
                        s_pe.inc(ins)

            @blk.vector
            def _(e):
                for p in range(n_pass):
                    for tc in range(TC):
                        idx = p * TC + tc
                        s_pe.wait(e, idx + 1)
                        bb = [banks[(tc % 4) * 2], banks[(tc % 4) * 2 + 1]]
                        consume_tc(e, p, tc, bb, s_ev)

            if dma_out is not None:
                stage, out_dram, s_st = dma_out

                @blk.scalar
                def _(e):
                    for p in range(n_pass):
                        for tc in range(TC):
                            for cg2 in range(2):
                                i = (p * TC + tc) * 2 + cg2
                                s_ev.wait(e, i + 1)
                                s_st.inc(i, e.dma_start(
                                    out=out_dram[
                                        128 * tc:128 * tc + 128,
                                        1024 * p + 512 * cg2:
                                        1024 * p + 512 * cg2 + 512],
                                    in_=stage[:, 512 * (i % 2):
                                              512 * (i % 2) + 512]))
                    s_st.wait_all(e)


def selection_consume_tc(kacc, vacc, ind, indtv, s_chain):
    def consume(e, p, tc, bb, s_ev):
        for cg2 in range(2):
            colbase = 1024 * p + 512 * cg2
            kpart = colbase < D
            acc = kacc if kpart else vacc
            wsel = ind if kpart else indtv
            for hl in range(4):
                h = ((colbase % D) // 128) + hl
                first = (colbase % D == 0 and hl == 0)
                a = acc[:, 128 * tc:128 * tc + 128]
                pp = bb[cg2][:, 128 * hl:128 * hl + 128]
                wcol = wsel[:, 16 * tc + h:16 * tc + h + 1]
                if first:
                    ins = e.tensor_scalar(a, pp, wcol, None, ALU.mult)
                else:
                    ins = e.scalar_tensor_tensor(a, pp, wcol, a,
                                                 ALU.mult, ALU.add)
                s_chain.inc(ins)
                s_chain.wait(e)
            s_ev.inc(e.engine_nop())
    return consume


def evac_stage_consume_tc(stage, s_st):
    ctr = [0]

    def consume(e, p, tc, bb, s_ev):
        for cg2 in range(2):
            i = ctr[0]
            ctr[0] += 1
            slot = i % 2
            s_st.wait_done(e, i - 2)
            s_ev.inc(e.tensor_copy(
                out=stage[:, 512 * slot:512 * slot + 512], in_=bb[cg2][:, :]))
    return consume


def evac_consume_tc(out_sb, width):
    def consume(e, p, tc, bb, s_ev):
        for cg2 in range(2):
            col = 1024 * p + 512 * cg2
            s_ev.inc(e.tensor_copy(
                out=out_sb[:, tc * width + col:tc * width + col + 512],
                in_=bb[cg2][:, :]))
    return consume


def selection_consume(kacc, vacc, ind, indtv, s_chain):
    def consume(e, cg, banks, s_ev):
        kpart = cg < 4
        acc = kacc if kpart else vacc
        wsel = ind if kpart else indtv
        for tc in range(TC):
            for hl in range(4):
                h = 4 * (cg % 4) + hl
                first = (cg % 4 == 0 and hl == 0)
                a = acc[:, 128 * tc:128 * tc + 128]
                p = banks[tc][:, 128 * hl:128 * hl + 128]
                wcol = wsel[:, 16 * tc + h:16 * tc + h + 1]
                if first:
                    ins = e.tensor_scalar(a, p, wcol, None, ALU.mult)
                else:
                    ins = e.scalar_tensor_tensor(a, p, wcol, a,
                                                 ALU.mult, ALU.add)
                s_chain.inc(ins)
                s_chain.wait(e)
                if hl == 3:
                    s_ev.inc(e.engine_nop())
    return consume


def evac_consume(out_sb, width):
    def consume(e, cg, banks, s_ev):
        for tc in range(TC):
            s_ev.inc(e.tensor_copy(
                out=out_sb[:, tc * width + 512 * cg:tc * width + 512 * cg + 512],
                in_=banks[tc][:, :]))
    return consume



def emit_flash2(nc, C, qhT, khT, kh_stride, vt, vt_stride, vt_hoff, masks,
                scale_post, fy_dtype, y_out_fn, add_src=None,
                side_section=None):
    with (
        nc.psum_tensor(_u("fS0"), [128, 1024], F32) as S0,
        nc.psum_tensor(_u("fS1"), [128, 1024], F32) as S1,
        nc.psum_tensor(_u("fY0"), [128, 512], F32) as Y0,
        nc.psum_tensor(_u("fY1"), [128, 512], F32) as Y1,
        nc.psum_tensor(_u("fL0"), [128, 512], F32) as L0,
        nc.psum_tensor(_u("fL1"), [128, 512], F32) as L1,
        nc.sbuf_tensor(_u("fpT"), [128, 4 * 1024], BF16) as pT,
        nc.sbuf_tensor(_u("frl"), [1, 2 * 512], F32) as rl,
        nc.sbuf_tensor(_u("frb"), [128, 2 * 512], F32) as rb,
        nc.sbuf_tensor(_u("ffy"), [128, 2 * 512], fy_dtype) as fy,
    ):
        Sp = [S0, S1]
        Yp = [Y0, Y1]
        Lp = [L0, L1]
        s_S = Sem(nc, "f_S")
        s_E = Sem(nc, "f_E")
        s_M = Sem(nc, "f_M")
        s_PV = Sem(nc, "f_PV")
        s_R = Sem(nc, "f_R")
        s_B = Sem(nc, "f_B")
        s_YE = Sem(nc, "f_YE")
        s_Y = DSem(nc, "f_Y")

        groups = []
        qblocks = []
        gidx = 0
        qidx = 0
        for hl in range(HPC):
            for qb in range(NQB):
                ng = 2 * qb + 2
                qblocks.append((hl, qb, ng, gidx, qidx))
                for g in range(ng):
                    groups.append((hl, qb, g, g >= ng - 2, gidx))
                    gidx += 1
                qidx += 1
        nmask_upto = {}
        acc = 0
        for (hl, qb, g, masked, gi) in groups:
            if masked:
                acc += 1
            nmask_upto[gi] = acc
        g2q = {}
        for (hl, qb, ng, g0, qi) in qblocks:
            for g in range(ng):
                g2q[g0 + g] = qi

        with nc.Block(_u("flash")) as blk:
            if side_section is not None:
                side_section(blk)

            @blk.tensor
            def _(e):
                def emit_S(gi):
                    (hl, qb, g, masked, _) = groups[gi]
                    sp = gi % 2
                    qs = hl * S + 512 * qb
                    ins = None
                    for half in range(2):
                        kc = 2 * g + half
                        kb = hl * kh_stride + 128 * kc
                        ins = e.matmul(Sp[sp][:, 512 * half:512 * half + 512],
                                       khT[:, kb:kb + 128],
                                       qhT[:, qs:qs + 512],
                                       start=True, stop=True)
                    s_S.inc(ins)

                def emit_Y(gi):
                    (hl, qb, g, masked, _) = groups[gi]
                    sp = gi % 4
                    qi = g2q[gi]
                    yb = qi % 2
                    ng = 2 * qb + 2
                    pos = g
                    if pos == 0 and qi >= 2:
                        s_YE.wait(e, 2 * (qi - 1))
                    for half in range(2):
                        kc = 2 * g + half
                        vb = kc * vt_stride + vt_hoff * hl
                        e.matmul(Yp[yb][:, :], vt[:, vb:vb + 128],
                                 pT[:, sp * 1024 + 512 * half:
                                    sp * 1024 + 512 * half + 512],
                                 start=(pos == 0 and half == 0),
                                 stop=(pos == ng - 1 and half == 1))

                def emit_L(gi):
                    (hl, qb, g, masked, _) = groups[gi]
                    sp = gi % 4
                    qi = g2q[gi]
                    yb = qi % 2
                    ng = 2 * qb + 2
                    pos = g
                    ins = None
                    for half in range(2):
                        ins = e.matmul(Lp[yb][0:1, :], C.ones_col_b[:, :],
                                       pT[:, sp * 1024 + 512 * half:
                                          sp * 1024 + 512 * half + 512],
                                       start=(pos == 0 and half == 0),
                                       stop=(pos == ng - 1 and half == 1))
                    if pos == ng - 1:
                        s_PV.inc(ins)

                def emit_BC(qi):
                    yb = qi % 2
                    s_R.wait(e, qi + 1)
                    s_B.inc(e.matmul(Lp[yb][:, :], C.ones_row_f[0:1, :],
                                     rl[0:1, 512 * yb:512 * yb + 512],
                                     start=True, stop=True))

                NG = len(groups)
                NB = NG // 2
                pending_bc = []
                def emit_S_batch(b):
                    if b >= NB:
                        return
                    for gi in (2 * b, 2 * b + 1):
                        if gi >= 2:
                            s_E.wait(e, gi - 1)
                        emit_S(gi)

                def emit_PV_batch(b):
                    if b == NB - 1:
                        s_E.wait(e, 2 * NB)
                    mneed = 0
                    for gi in (2 * b, 2 * b + 1):
                        if groups[gi][3]:
                            mneed = max(mneed, nmask_upto[gi])
                    if mneed:
                        s_M.wait(e, mneed)
                    for gi in (2 * b, 2 * b + 1):
                        emit_Y(gi)
                    for gi in (2 * b, 2 * b + 1):
                        emit_L(gi)

                emit_S_batch(0)
                for b in range(NB):
                    emit_S_batch(b + 1)
                    while pending_bc:
                        emit_BC(pending_bc.pop(0))
                    emit_PV_batch(b)
                    if groups[2 * b + 1][2] == 2 * groups[2 * b + 1][1] + 1:
                        pending_bc.append(g2q[2 * b + 1])
                while pending_bc:
                    emit_BC(pending_bc.pop(0))

            @blk.scalar
            def _(e):
                for (hl, qb, g, masked, gi) in groups:
                    sp4 = gi % 4
                    s_S.wait(e, gi + 1)
                    s_E.inc(e.activation(out=pT[:, sp4 * 1024:sp4 * 1024 + 1024],
                                         in_=Sp[gi % 2][:, :], func=AF.Exp,
                                         bias=C.bias_sh[:, 0:1],
                                         scale=SM_SCALE))

            @blk.vector
            def _(e):
                s_rl = Sem(nc, "f_rl")
                for (hl, qb, ng, g0, qi) in qblocks:
                    yb = qi % 2
                    for gi in (g0 + ng - 2, g0 + ng - 1):
                        g = groups[gi][2]
                        sp = gi % 4
                        s_E.wait(e, gi + 1)
                        mask = masks[0] if g == ng - 2 else masks[1]
                        s_M.inc(e.tensor_mul(
                            pT[:, sp * 1024:sp * 1024 + 1024],
                            pT[:, sp * 1024:sp * 1024 + 1024], mask[:, :]))
                    s_PV.wait(e, qi + 1)
                    s_rl.inc(e.reciprocal(rl[0:1, 512 * yb:512 * yb + 512],
                                          Lp[yb][0:1, :]))
                    s_rl.wait(e)
                    if scale_post != 1.0:
                        s_rl.inc(e.tensor_scalar_mul(
                            rl[0:1, 512 * yb:512 * yb + 512],
                            rl[0:1, 512 * yb:512 * yb + 512],
                            float(scale_post)))
                        s_rl.wait(e)
                    s_R.inc(e.engine_nop())
                    s_B.wait(e, qi + 1)
                    s_Y.wait_done(e, qi - 2)
                    s_YE.inc(e.tensor_copy(out=rb[:, 512 * yb:512 * yb + 512],
                                           in_=Lp[yb][:, :]))
                    s_YE.wait(e)
                    if add_src is None:
                        s_YE.inc(e.tensor_mul(fy[:, 512 * yb:512 * yb + 512],
                                              Yp[yb][:, :],
                                              rb[:, 512 * yb:512 * yb + 512]))
                    else:
                        s_rl.inc(e.tensor_mul(rb[:, 512 * yb:512 * yb + 512],
                                              Yp[yb][:, :],
                                              rb[:, 512 * yb:512 * yb + 512]))
                        s_rl.wait(e)
                        s_YE.inc(e.tensor_add(
                            fy[:, 512 * yb:512 * yb + 512],
                            add_src[:, hl * S + 512 * qb:hl * S + 512 * qb + 512],
                            rb[:, 512 * yb:512 * yb + 512]))

            @blk.sync
            def _(e):
                for (hl, qb, ng, g0, qi) in qblocks:
                    yb = qi % 2
                    s_YE.wait(e, 2 * (qi + 1))
                    y_out_fn(e, hl, qb, qi, fy[:, 512 * yb:512 * yb + 512], s_Y)
                s_Y.wait_all(e)


def emit_transpose_seq(nc, C, src, dst, n, ident):
    with (
        nc.psum_tensor(_u("ptr0"), [128, 128], BF16) as q0,
        nc.psum_tensor(_u("ptr1"), [128, 128], BF16) as q1,
    ):
        qp = [q0, q1]
        s_t = Sem(nc, "tr_t")
        s_c = Sem(nc, "tr_c")
        with nc.Block(_u("transpose")) as blk:
            @blk.tensor
            def _(e):
                for j in range(n):
                    par = j % 2
                    if j >= 2:
                        s_c.wait(e, j - 1)
                    s_t.inc(e.transpose(qp[par][:, :],
                                        src[:, 128 * j:128 * j + 128],
                                        ident[:, :]))

            @blk.vector
            def _(e):
                for j in range(n):
                    par = j % 2
                    s_t.wait(e, j + 1)
                    s_c.inc(e.tensor_copy(out=dst[:, 128 * j:128 * j + 128],
                                          in_=qp[par][:, :]))




def make_side_cast(nc, pairs, chunk_cols=2048):
    scf = nc.alloc_sbuf_tensor(_u("scf"), [128, 2 * chunk_cols], F32)
    scb = nc.alloc_sbuf_tensor(_u("scb"), [128, 2 * chunk_cols], BF16)
    s_d = DSem(nc, "sc_d")
    s_c = Sem(nc, "sc_c")
    s_o = DSem(nc, "sc_o")
    jobs = []
    for (srcp, dstp) in pairs:
        R, Cc = srcp.shape
        assert Cc == chunk_cols and R % 128 == 0
        for r in range(R // 128):
            jobs.append((srcp, dstp, r))

    def section(blk):
        @blk.gpsimd
        def _(e):
            def load(i):
                if i >= len(jobs):
                    return
                srcp, dstp, r = jobs[i]
                s_d.inc(i, e.dma_start(
                    out=scf[:, (i % 2) * chunk_cols:(i % 2 + 1) * chunk_cols],
                    in_=srcp[128 * r:128 * r + 128, :]))
            load(0)
            load(1)
            for i, (srcp, dstp, r) in enumerate(jobs):
                par = i % 2
                s_d.wait_done(e, i)
                s_o.wait_done(e, i - 2)
                s_c.inc(e.tensor_copy(
                    out=scb[:, par * chunk_cols:(par + 1) * chunk_cols],
                    in_=scf[:, par * chunk_cols:(par + 1) * chunk_cols]))
                s_c.wait(e)
                s_o.inc(i, e.dma_start(
                    out=dstp[128 * r:128 * r + 128, :],
                    in_=scb[:, par * chunk_cols:(par + 1) * chunk_cols]))
                load(i + 2)
            s_o.wait_all(e)
    return section



def build_prep1(with_carry=False):
    nc = bass.Bass("TRN2", target_bir_lowering=False)
    C = Consts(nc)
    x_t = nc.dram_tensor("x_t", [D, T], F32, kind="ExternalInput").ap()
    wg = nc.dram_tensor("wg", [D, 16], F32, kind="ExternalInput").ap()
    WDT = BF16 if with_carry else F32
    wkv = nc.dram_tensor("wkv", [D, 2 * D], WDT, kind="ExternalInput").ap()
    wq = nc.dram_tensor("wq", [D, D], WDT, kind="ExternalInput").ap()
    if with_carry:
        y2_t = nc.dram_tensor("y2_t", [D, T], BF16, kind="ExternalInput").ap()
        wo = nc.dram_tensor("wo", [D, D], BF16, kind="ExternalInput").ap()
        x2_t = nc.dram_tensor("x2_t", [D, T], F32, kind="ExternalOutput").ap()
    qt = nc.dram_tensor("qt", [D, T], BF16, kind="ExternalOutput").ap()
    ksht = nc.dram_tensor("ksht", [128, T], BF16, kind="ExternalOutput").ap()
    vsht = nc.dram_tensor("vsht", [T, 128], BF16, kind="ExternalOutput").ap()
    idn = nc.dram_tensor("idn", [128, 128], F32, kind="ExternalInput").ap()

    xT = nc.alloc_sbuf_tensor("xT", [128, NK * T], F32)
    xlnT = nc.alloc_sbuf_tensor("xlnT", [128, NK * T], BF16)
    mrcol = nc.alloc_sbuf_tensor("mrcol", [128, 16], F32)
    wg_sb = nc.alloc_sbuf_tensor("wg_sb", [128, NK * 16], F32)
    ind = nc.alloc_sbuf_tensor("ind_sb", [128, TC * 16], F32)
    indtv = nc.alloc_sbuf_tensor("indtv_sb", [128, TC * 16], F32)
    kacc = nc.alloc_sbuf_tensor("kacc", [128, TC * 128], F32)
    vacc = nc.alloc_sbuf_tensor("vacc", [128, TC * 128], F32)
    kvb = nc.alloc_sbuf_tensor("kvb", [128, TC * 128], BF16)
    ksb = nc.alloc_sbuf_tensor("ksb", [128, T], BF16)
    identb = nc.alloc_sbuf_tensor("identb", [128, 128], F32)

    C.emit(nc)
    loads = [
        (xT[:, :].rearrange("p (k t) -> p k t", t=T),
         x_t[:, :].rearrange("(k p) t -> p k t", p=128)),
        (wg_sb[:, :].rearrange("p (k h) -> p k h", h=16),
         wg[:, :].rearrange("(k p) h -> p k h", p=128)),
        (identb[:, :], idn[:, :]),
    ]
    if with_carry:
        loads.append((xlnT[:, :].rearrange("p (k t) -> p k t", t=T),
                      y2_t[:, :].rearrange("(k p) t -> p k t", p=128)))
        wres = proj_tok_res_alloc(nc)
        s_wkv = DSem(nc, "wkv")
    emit_loads(nc, loads)

    if with_carry:
        proj_tok_res_prefetch(nc, wkv, wres, s_wkv)
        emit_proj_feat(nc, C, wo, NK, xlnT, None, out_f32_add=xT,
                       out_dram_f32=x2_t)

    emit_ln(nc, C, xT, xlnT, mrcol,
            gate=dict(wg_sb=wg_sb, ind=ind, indtv=indtv))

    s_chain = Sem(nc, "selchain")
    if with_carry:
        emit_proj_tok_res(nc, C, wkv, xlnT,
                          selection_consume_tc(kacc, vacc, ind, indtv, s_chain),
                          wres, s_wkv)
    else:
        emit_proj_tok(nc, C, wkv, 8, xlnT,
                      selection_consume(kacc, vacc, ind, indtv, s_chain))

    s_kv = Sem(nc, "kv_cast")
    s_kvs = Sem(nc, "kv_st")
    s_ktr = Sem(nc, "kv_tr")
    with (
        nc.psum_tensor(_u("ptk0"), [128, 128], F32) as ptk0,
        nc.psum_tensor(_u("ptk1"), [128, 128], F32) as ptk1,
    ):
        ptk = [ptk0, ptk1]
        with nc.Block(_u("kv_out")) as blk:
            @blk.vector
            def _(e):
                for tc in range(TC):
                    s_kv.inc(e.tensor_copy(
                        out=kvb[:, 128 * tc:128 * tc + 128],
                        in_=vacc[:, 128 * tc:128 * tc + 128]))
                for tc in range(TC):
                    s_ktr.wait(e, tc + 1)
                    s_kv.inc(e.tensor_copy(
                        out=ksb[:, 128 * tc:128 * tc + 128],
                        in_=ptk[tc % 2][:, :]))

            @blk.tensor
            def _(e):
                for tc in range(TC):
                    if tc >= 2:
                        s_kv.wait(e, 8 + tc - 1)
                    s_ktr.inc(e.transpose(ptk[tc % 2][:, :],
                                          kacc[:, 128 * tc:128 * tc + 128],
                                          identb[:, :]))

            @blk.sync
            def _(e):
                s_kv.wait(e)
                s_kvs.inc(e.dma_start(
                    out=vsht[:, :].rearrange("(tc p) c -> p tc c", p=128),
                    in_=kvb[:, :].rearrange("p (tc c) -> p tc c", c=128)), 16)
                s_kvs.inc(e.dma_start(out=ksht[:, :], in_=ksb[:, :]), 16)
                s_kvs.wait(e)

    emit_proj_feat(nc, C, wq, NK, xlnT, qt)
    nc.finalize()
    return nc


def build_prep2():
    nc = bass.Bass("TRN2", target_bir_lowering=False)
    C = Consts(nc)
    x_t = nc.dram_tensor("x_t", [D, T], F32, kind="ExternalInput").ap()
    y1_t = nc.dram_tensor("y1_t", [D, T], F32, kind="ExternalInput").ap()
    wq = nc.dram_tensor("wq", [D, D], BF16, kind="ExternalInput").ap()
    wk = nc.dram_tensor("wk", [D, D], BF16, kind="ExternalInput").ap()
    wv = nc.dram_tensor("wv", [D, D], BF16, kind="ExternalInput").ap()
    x1_t = nc.dram_tensor("x1_t", [D, T], F32, kind="ExternalOutput").ap()
    q2t = nc.dram_tensor("q2t", [D, T], BF16, kind="ExternalOutput").ap()
    k2t = nc.dram_tensor("k2t", [D, T], BF16, kind="ExternalOutput").ap()
    v2t = nc.dram_tensor("v2t", [T, D], BF16, kind="ExternalOutput").ap()

    xT = nc.alloc_sbuf_tensor("xT", [128, NK * T], F32)
    xlnT = nc.alloc_sbuf_tensor("xlnT", [128, NK * T], BF16)
    mrcol = nc.alloc_sbuf_tensor("mrcol", [128, 16], F32)
    ybuf = nc.alloc_sbuf_tensor("ybuf", [128, 2 * T], F32)
    vstg = nc.alloc_sbuf_tensor("vstg", [128, 2 * 512], BF16)

    C.emit(nc)
    wres = proj_tok_res_alloc(nc)
    s_wv = DSem(nc, "wv")
    emit_loads(nc, [
        (xT[:, :].rearrange("p (k t) -> p k t", t=T),
         x_t[:, :].rearrange("(k p) t -> p k t", p=128)),
    ])
    proj_tok_res_prefetch(nc, wv, wres, s_wv)

    s_yd = DSem(nc, "x1_yd")
    s_ya = Sem(nc, "x1_ya")
    s_ys = DSem(nc, "x1_st")
    with nc.Block(_u("x1_add")) as blk:
        @blk.sync
        def _(e):
            for k in range(NK):
                par = k % 2
                if k >= 2:
                    s_ya.wait(e, k - 1)
                s_yd.inc(k, e.dma_start(
                    out=ybuf[:, par * T:(par + 1) * T],
                    in_=y1_t[128 * k:128 * k + 128, :]))

        @blk.vector
        def _(e):
            for k in range(NK):
                par = k % 2
                s_yd.wait_done(e, k)
                s_ya.inc(e.tensor_add(xT[:, k * T:(k + 1) * T],
                                      xT[:, k * T:(k + 1) * T],
                                      ybuf[:, par * T:(par + 1) * T]))
                s_ya.wait(e)

        @blk.sync
        def _(e):
            for k in range(NK):
                s_ya.wait(e, k + 1)
                s_ys.inc(k, e.dma_start(out=x1_t[128 * k:128 * k + 128, :],
                                        in_=xT[:, k * T:(k + 1) * T]))
            s_ys.wait_all(e)

    emit_ln(nc, C, xT, xlnT, mrcol, gate=None)
    emit_proj_feat(nc, C, wq, NK, xlnT, q2t)
    emit_proj_feat(nc, C, wk, NK, xlnT, k2t)
    s_v2st = DSem(nc, "v2st")
    emit_proj_tok_res(nc, C, wv, xlnT,
                      evac_stage_consume_tc(vstg, s_v2st), wres, s_wv,
                      dma_out=(vstg, v2t, s_v2st))
    nc.finalize()
    return nc


def build_flash_attn():
    nc = bass.Bass("TRN2", target_bir_lowering=False)
    C = Consts(nc)
    qh = nc.dram_tensor("qh", [HPC * 128, S], BF16, kind="ExternalInput").ap()
    kh = nc.dram_tensor("kh", [HPC * 128, S], BF16, kind="ExternalInput").ap()
    vh = nc.dram_tensor("vh", [S, HPC * 128], BF16, kind="ExternalInput").ap()
    msk = nc.dram_tensor("msk", [128, 2048], BF16, kind="ExternalInput").ap()
    yt = nc.dram_tensor("yt", [HPC * 128, S], BF16, kind="ExternalOutput").ap()
    wo_f = nc.dram_tensor("wo_f", [D // W, D], F32, kind="ExternalInput").ap()
    wo_b = nc.dram_tensor("wo_b", [D // W, D], BF16, kind="ExternalOutput").ap()

    qhT = nc.alloc_sbuf_tensor("qhT", [128, HPC * S], BF16)
    khT = nc.alloc_sbuf_tensor("khT", [128, HPC * S], BF16)
    vt = nc.alloc_sbuf_tensor("vt", [128, (S // 128) * 256], BF16)
    masks = nc.alloc_sbuf_tensor("masks", [128, 2048], BF16)

    C.emit(nc)
    side_section = make_side_cast(nc, [(wo_f, wo_b)])
    emit_loads(nc, [
        (qhT[:, :].rearrange("p (hl t) -> p hl t", t=S),
         qh[:, :].rearrange("(hl p) t -> p hl t", p=128)),
        (khT[:, :].rearrange("p (hl t) -> p hl t", t=S),
         kh[:, :].rearrange("(hl p) t -> p hl t", p=128)),
        (vt[:, :].rearrange("p (j c) -> p j c", c=256),
         vh[:, :].rearrange("(j p) c -> p j c", p=128)),
        (masks[:, :], msk[:, :]),
    ])

    def y_out(e, hl, qb, qi, src, s_Y):
        s_Y.inc(qi, e.dma_start(
            out=yt[128 * hl:128 * hl + 128, 512 * qb:512 * qb + 512],
            in_=src))

    emit_flash2(nc, C, qhT, khT, S, vt, 256, 128, (masks[:, 0:1024], masks[:, 1024:2048]),
                1.0, BF16, y_out, side_section=side_section)
    nc.finalize()
    return nc


def build_flash_ffn(final=False):
    nc = bass.Bass("TRN2", target_bir_lowering=False)
    C = Consts(nc)
    qh = nc.dram_tensor("qh", [HPC * 128, S], BF16, kind="ExternalInput").ap()
    khx = nc.dram_tensor("khx", [128, S], BF16, kind="ExternalInput").ap()
    vv = nc.dram_tensor("vv", [S, 128], BF16, kind="ExternalInput").ap()
    msk = nc.dram_tensor("msk", [128, 2048], BF16, kind="ExternalInput").ap()
    side = []
    if not final:
        R8 = D // W
        wq_f = nc.dram_tensor("wq_f", [R8, D], F32, kind="ExternalInput").ap()
        wk_f = nc.dram_tensor("wk_f", [R8, D], F32, kind="ExternalInput").ap()
        wv_f = nc.dram_tensor("wv_f", [R8, D], F32, kind="ExternalInput").ap()
        wkv_f = nc.dram_tensor("wkv_f", [R8, 2 * D], F32, kind="ExternalInput").ap()
        wq1_f = nc.dram_tensor("wq1_f", [R8, D], F32, kind="ExternalInput").ap()
        wq_b = nc.dram_tensor("wq_b", [R8, D], BF16, kind="ExternalOutput").ap()
        wk_b = nc.dram_tensor("wk_b", [R8, D], BF16, kind="ExternalOutput").ap()
        wv_b = nc.dram_tensor("wv_b", [R8, D], BF16, kind="ExternalOutput").ap()
        wkv_b = nc.dram_tensor("wkv_b", [R8, 2 * D], BF16, kind="ExternalOutput").ap()
        wq1_b = nc.dram_tensor("wq1_b", [R8, D], BF16, kind="ExternalOutput").ap()
        side = [(wq_f, wq_b), (wk_f, wk_b), (wv_f, wv_b),
                (wkv_f[:, 0:D], wkv_b[:, 0:D]),
                (wkv_f[:, D:2 * D], wkv_b[:, D:2 * D]),
                (wq1_f, wq1_b)]
    if final:
        x2h = nc.dram_tensor("x2h", [HPC * 128, S], F32, kind="ExternalInput").ap()
        x3h = nc.dram_tensor("x3h", [HPC * 128, S], F32, kind="ExternalOutput").ap()
    else:
        yt = nc.dram_tensor("yt", [HPC * 128, S], F32, kind="ExternalOutput").ap()

    qhT = nc.alloc_sbuf_tensor("qhT", [128, HPC * S], BF16)
    khT = nc.alloc_sbuf_tensor("khT", [128, S], BF16)
    vt = nc.alloc_sbuf_tensor("vt", [128, S], BF16)
    masks = nc.alloc_sbuf_tensor("masks", [128, 2048], BF16)

    C.emit(nc)
    side_section = make_side_cast(nc, side) if side else None
    if final:
        x2sb = nc.alloc_sbuf_tensor("x2sb", [128, HPC * S], F32)
    loads = [
        (khT[:, :], khx[:, :]),
        (qhT[:, :].rearrange("p (hl t) -> p hl t", t=S),
         qh[:, :].rearrange("(hl p) t -> p hl t", p=128)),
        (vt[:, :].rearrange("p (j c) -> p j c", c=128),
         vv[:, :].rearrange("(j p) c -> p j c", p=128)),
        (masks[:, :], msk[:, :]),
    ]
    if final:
        loads.append((x2sb[:, :].rearrange("p (hl t) -> p hl t", t=S),
                      x2h[:, :].rearrange("(hl p) t -> p hl t", p=128)))
    emit_loads(nc, loads)

    ydst = x3h if final else yt

    def y_out(e, hl, qb, qi, src, s_Y):
        s_Y.inc(qi, e.dma_start(
            out=ydst[128 * hl:128 * hl + 128, 512 * qb:512 * qb + 512],
            in_=src))

    emit_flash2(nc, C, qhT, khT, 0, vt, 128, 0,
                (masks[:, 0:1024], masks[:, 1024:2048]), 0.5, F32, y_out,
                add_src=(x2sb if final else None), side_section=side_section)

    nc.finalize()
    return nc



def _causal_masks():
    m = np.zeros((128, 2048), np.float32)
    for half, moff in ((0, 0), (1, 1), (2, 2), (3, 3)):
        j = np.arange(128)[:, None]
        i = np.arange(512)[None, :]
        m[:, 512 * half:512 * half + 512] = (i >= j + 128 * moff)
    return m.astype(BF)


def _run(nc, in_maps):
    res = run_bass_kernel_spmd(nc, in_maps, core_ids=list(range(W)))
    return res.results


_NC_CACHE = {}


def _get(name, builder, *a, **k):
    if name not in _NC_CACHE:
        _NC_CACHE[name] = builder(*a, **k)
    return _NC_CACHE[name]


def kernel(L0_attn_x, L0_attn_Wq, L0_attn_Wk, L0_attn_Wv, L0_attn_Wo,
           L0_ffn_W_gate, L0_ffn_W_qkv):
    x = np.ascontiguousarray(np.asarray(L0_attn_x, np.float32)[0])
    XT = np.ascontiguousarray(x.T)
    Wq = np.asarray(L0_attn_Wq, np.float32)
    Wk = np.asarray(L0_attn_Wk, np.float32)
    Wv = np.asarray(L0_attn_Wv, np.float32)
    Wo = np.asarray(L0_attn_Wo, np.float32)
    Wg = np.asarray(L0_ffn_W_gate, np.float32)
    Wqkv = np.asarray(L0_ffn_W_qkv, np.float32)

    wgT = np.ascontiguousarray(Wg.T)
    wq1T = np.ascontiguousarray(Wqkv[0:D].T)
    wkvT = np.ascontiguousarray(Wqkv[D:3 * D].T)
    wqT = np.ascontiguousarray(Wq.T)
    wkT = np.ascontiguousarray(Wk.T)
    wvT = np.ascontiguousarray(Wv.T)
    woT = np.ascontiguousarray(Wo.T)
    msk = _causal_masks()
    idn = np.eye(128, dtype=np.float32)

    def xslice(A, c):
        return np.ascontiguousarray(A[:, T * c:T * c + T])

    nc1 = _get("prep1", build_prep1, False)
    r1 = _run(nc1, [{"x_t": xslice(XT, c), "wg": wgT, "wkv": wkvT,
                     "wq": wq1T, "idn": idn} for c in range(W)])
    Q1 = np.concatenate([r1[c]["qt"] for c in range(W)], axis=1)
    KT1 = np.concatenate([r1[c]["ksht"] for c in range(W)], axis=1)
    VS1 = np.concatenate([r1[c]["vsht"] for c in range(W)], axis=0)

    nc2 = _get("ffn_flash", build_flash_ffn, False)
    R8 = D // W
    r2 = _run(nc2, [{"qh": np.ascontiguousarray(Q1[256 * c:256 * c + 256]),
                     "khx": KT1, "vv": VS1, "msk": msk,
                     "wq_f": np.ascontiguousarray(wqT[R8 * c:R8 * c + R8]),
                     "wk_f": np.ascontiguousarray(wkT[R8 * c:R8 * c + R8]),
                     "wv_f": np.ascontiguousarray(wvT[R8 * c:R8 * c + R8]),
                     "wkv_f": np.ascontiguousarray(wkvT[R8 * c:R8 * c + R8]),
                     "wq1_f": np.ascontiguousarray(wq1T[R8 * c:R8 * c + R8])}
                    for c in range(W)])
    Y1 = np.concatenate([r2[c]["yt"] for c in range(W)], axis=0)
    wqB = np.concatenate([r2[c]["wq_b"] for c in range(W)], axis=0)
    wkB = np.concatenate([r2[c]["wk_b"] for c in range(W)], axis=0)
    wvB = np.concatenate([r2[c]["wv_b"] for c in range(W)], axis=0)
    wkvB = np.concatenate([r2[c]["wkv_b"] for c in range(W)], axis=0)
    wq1B = np.concatenate([r2[c]["wq1_b"] for c in range(W)], axis=0)

    nc3 = _get("prep2", build_prep2)
    r3 = _run(nc3, [{"x_t": xslice(XT, c), "y1_t": xslice(Y1, c),
                     "wq": wqB, "wk": wkB, "wv": wvB} for c in range(W)])
    X1 = np.concatenate([r3[c]["x1_t"] for c in range(W)], axis=1)
    Q2 = np.concatenate([r3[c]["q2t"] for c in range(W)], axis=1)
    K2 = np.concatenate([r3[c]["k2t"] for c in range(W)], axis=1)
    V2 = np.concatenate([r3[c]["v2t"] for c in range(W)], axis=0)

    nc4 = _get("attn_flash", build_flash_attn)
    r4 = _run(nc4, [{"qh": np.ascontiguousarray(Q2[256 * c:256 * c + 256]),
                     "kh": np.ascontiguousarray(K2[256 * c:256 * c + 256]),
                     "vh": np.ascontiguousarray(V2[:, 256 * c:256 * c + 256]),
                     "msk": msk,
                     "wo_f": np.ascontiguousarray(woT[R8 * c:R8 * c + R8])}
                    for c in range(W)])
    Y2 = np.concatenate([r4[c]["yt"] for c in range(W)], axis=0)
    woB = np.concatenate([r4[c]["wo_b"] for c in range(W)], axis=0)

    nc5 = _get("prep3", build_prep1, True)
    r5 = _run(nc5, [{"x_t": xslice(X1, c), "y2_t": xslice(Y2, c),
                     "wo": woB, "wg": wgT, "wkv": wkvB, "wq": wq1B,
                     "idn": idn}
                    for c in range(W)])
    X2 = np.concatenate([r5[c]["x2_t"] for c in range(W)], axis=1)
    Q3 = np.concatenate([r5[c]["qt"] for c in range(W)], axis=1)
    KT3 = np.concatenate([r5[c]["ksht"] for c in range(W)], axis=1)
    VS3 = np.concatenate([r5[c]["vsht"] for c in range(W)], axis=0)

    nc6 = _get("ffn_flash_fin", build_flash_ffn, True)
    r6 = _run(nc6, [{"qh": np.ascontiguousarray(Q3[256 * c:256 * c + 256]),
                     "khx": KT3, "vv": VS3, "msk": msk,
                     "x2h": np.ascontiguousarray(X2[256 * c:256 * c + 256])}
                    for c in range(W)])
    X3 = np.concatenate([r6[c]["x3h"] for c in range(W)], axis=0)

    return np.ascontiguousarray(X3.T).reshape(1, S, D)


# version 16
# speedup vs baseline: 1.0149x; 1.0149x over previous
import numpy as np
import ml_dtypes
import concourse.bass as bass
import concourse.mybir as mybir
from concourse.bass_utils import run_bass_kernel_spmd

F32 = mybir.dt.float32
BF16 = mybir.dt.bfloat16
AF = mybir.ActivationFunctionType
ALU = mybir.AluOpType
AX = mybir.AxisListType
BF = ml_dtypes.bfloat16

S, D, H, HD, W = 8192, 2048, 16, 128, 8
T = S // W
HPC = H // W
NK = D // 128
TC = T // 128
NQB = S // 512
EPS = 1e-5
SM_SCALE = float(1.0 / np.sqrt(HD))
SHIFT = -20.0

_uid = [0]


def _u(name):
    _uid[0] += 1
    return f"{name}_{_uid[0]}"


class Sem:
    def __init__(self, nc, name):
        self.h = nc.alloc_semaphore(_u(name))
        self.v = 0

    def inc(self, instr, k=1):
        instr.then_inc(self.h, k)
        self.v += k
        return self.v

    def wait(self, eng, v=None):
        eng.wait_ge(self.h, self.v if v is None else v)


class DSem:

    def __init__(self, nc, name):
        self.s = [Sem(nc, name + "a"), Sem(nc, name + "b")]

    def inc(self, i, instr):
        return self.s[i % 2].inc(instr, 16)

    def wait_done(self, eng, i):
        if i < 0:
            return
        self.s[i % 2].wait(eng, 16 * (i // 2 + 1))

    def wait_all(self, eng):
        self.s[0].wait(eng)
        self.s[1].wait(eng)


class DSem4:

    def __init__(self, nc, name):
        self.s = [Sem(nc, name + str(i)) for i in range(4)]

    def inc(self, i, instr):
        return self.s[i % 4].inc(instr, 16)

    def wait_done(self, eng, i):
        if i < 0:
            return
        self.s[i % 4].wait(eng, 16 * (i // 4 + 1))

    def wait_all(self, eng):
        for s in self.s:
            s.wait(eng)


class Consts:
    def __init__(self, nc):
        self.ones_col_f = nc.alloc_sbuf_tensor(_u("c_ocf"), [128, 1], F32)
        self.ones_col_b = nc.alloc_sbuf_tensor(_u("c_ocb"), [128, 1], BF16)
        self.ones_row_f = nc.alloc_sbuf_tensor(_u("c_orf"), [1, 128], F32)
        self.one1_f = nc.alloc_sbuf_tensor(_u("c_o1f"), [1, 1], F32)
        self.bias_sh = nc.alloc_sbuf_tensor(_u("c_bsh"), [128, 1], F32)
        self.eps_t = nc.alloc_sbuf_tensor(_u("c_eps"), [1, 1], F32)

    def emit(self, nc):
        with nc.Block(_u("consts")) as blk:
            @blk.vector
            def _(e):
                e.memset(self.ones_col_f[:, :], 1.0)
                e.memset(self.ones_col_b[:, :], 1.0)
                e.memset(self.ones_row_f[:, :], 1.0)
                e.memset(self.one1_f[:, :], 1.0)
                e.memset(self.bias_sh[:, :], SHIFT)
                e.memset(self.eps_t[:, :], EPS)


def emit_loads(nc, loads, name="inputs"):
    s = Sem(nc, "s_in")
    with nc.Block(name) as blk:
        @blk.sync
        def _(e):
            for out_ap, in_ap in loads:
                s.inc(e.dma_start(out=out_ap, in_=in_ap), 16)
            s.wait(e)



def emit_ln(nc, C, xT, xlnT, mrcol, gate=None):
    import contextlib
    with (
        nc.sbuf_tensor(_u("mB"), [128, 1024], F32) as mB,
        nc.sbuf_tensor(_u("rB"), [128, 1024], F32) as rB,
        nc.sbuf_tensor(_u("tmp_n"), [128, (1 if gate else 2) * 1024],
                       F32) as tmp_n,
    ):
        _ps = contextlib.ExitStack()
        sq_sc = _ps.enter_context(
            nc.sbuf_tensor(_u("sq_sc"), [128, 2 * 1024], BF16))
        xb_sc = _ps.enter_context(
            nc.sbuf_tensor(_u("xb_sc"), [128, 2 * 1024], BF16))
        rows = _ps.enter_context(
            nc.sbuf_tensor(_u("rows"), [1, 3 * 1024], F32))
        p_sx = _ps.enter_context(nc.psum_tensor(_u("p_sx"), [1, 1024], F32))
        p_sq = _ps.enter_context(nc.psum_tensor(_u("p_sq"), [1, 1024], F32))
        p_b0 = _ps.enter_context(nc.psum_tensor(_u("p_b0"), [128, 512], F32))
        p_b1 = _ps.enter_context(nc.psum_tensor(_u("p_b1"), [128, 512], F32))
        p_mr = _ps.enter_context(nc.psum_tensor(_u("p_mr"), [128, 16], F32))
        s_sq = Sem(nc, "ln_sq")
        s_pe = Sem(nc, "ln_pe")
        s_ve = Sem(nc, "ln_ve")
        s_se = Sem(nc, "ln_se")

        with nc.Block(_u("ln_stats")) as blk:
            @blk.scalar
            def _(e):
                for k in range(NK):
                    if k >= 2:
                        s_pe.wait(e, 2 * (k - 1))
                    e.activation(
                        out=xb_sc[:, (k % 2) * 1024:(k % 2) * 1024 + 1024],
                        in_=xT[:, k * T:k * T + T], func=AF.Copy)
                    s_sq.inc(e.activation(
                        out=sq_sc[:, (k % 2) * 1024:(k % 2) * 1024 + 1024],
                        in_=xT[:, k * T:k * T + T], func=AF.Square))

            @blk.tensor
            def _(e):
                for k in range(NK):
                    st, sp = (k == 0), (k == NK - 1)
                    s_sq.wait(e, k + 1)
                    e.matmul(p_sx[0:1, 0:512], C.ones_col_b[:, :],
                             xb_sc[:, (k % 2) * 1024:(k % 2) * 1024 + 512],
                             start=st, stop=sp)
                    e.matmul(p_sx[0:1, 512:1024], C.ones_col_b[:, :],
                             xb_sc[:, (k % 2) * 1024 + 512:(k % 2) * 1024 + 1024],
                             start=st, stop=sp)
                    e.matmul(p_sq[0:1, 0:512], C.ones_col_b[:, :],
                             sq_sc[:, (k % 2) * 1024:(k % 2) * 1024 + 512],
                             start=st, stop=sp)
                    ins = e.matmul(
                        p_sq[0:1, 512:1024], C.ones_col_b[:, :],
                        sq_sc[:, (k % 2) * 1024 + 512:(k % 2) * 1024 + 1024],
                        start=st, stop=sp)
                    s_pe.inc(ins, 2)

        m_row = rows[0:1, 0:1024]
        r_row = rows[0:1, 1024:2048]
        ex2 = rows[0:1, 2048:3072]
        m2 = rows[0:1, 1024:2048]
        var = rows[0:1, 2048:3072]
        sd = rows[0:1, 1024:2048]
        with nc.Block(_u("ln_fin")) as blk:
            @blk.vector
            def _(e):
                s_ve.inc(e.tensor_scalar_mul(m_row, p_sx[0:1, :], 1.0 / D))
                s_ve.inc(e.tensor_scalar_mul(ex2, p_sq[0:1, :], 1.0 / D))
                s_ve.wait(e)
                s_ve.inc(e.tensor_mul(m2, m_row, m_row))
                s_ve.wait(e)
                s_ve.inc(e.tensor_sub(var, ex2, m2))

            @blk.scalar
            def _(e):
                s_ve.wait(e)
                s_se.inc(e.activation(out=sd, in_=var, func=AF.Sqrt,
                                      bias=C.eps_t[0:1, 0:1]))

            @blk.vector
            def _(e):
                s_se.wait(e)
                s_ve.inc(e.reciprocal(r_row, sd))

        with nc.Block(_u("ln_bc")) as blk:
            @blk.tensor
            def _(e):
                for tc in range(TC):
                    e.matmul(p_mr[:, tc:tc + 1],
                             m_row[0:1, 128 * tc:128 * tc + 128],
                             C.one1_f[0:1, 0:1], start=True, stop=True)
                    e.matmul(p_mr[:, TC + tc:TC + tc + 1],
                             r_row[0:1, 128 * tc:128 * tc + 128],
                             C.one1_f[0:1, 0:1], start=True, stop=True)
                e.matmul(p_b0[:, :], C.ones_row_f[0:1, :], m_row[0:1, 0:512],
                         start=True, stop=True)
                s_pe.inc(e.matmul(p_b1[:, :], C.ones_row_f[0:1, :],
                                  m_row[0:1, 512:1024], start=True, stop=True))

            @blk.vector
            def _(e):
                s_pe.wait(e)
                s_ve.inc(e.tensor_copy(out=mrcol[:, :], in_=p_mr[:, :]))
                s_ve.inc(e.tensor_copy(out=mB[:, 0:512], in_=p_b0[:, :]))
                s_ve.inc(e.tensor_copy(out=mB[:, 512:1024], in_=p_b1[:, :]))

            @blk.tensor
            def _(e):
                s_ve.wait(e)
                e.matmul(p_b0[:, :], C.ones_row_f[0:1, :], r_row[0:1, 0:512],
                         start=True, stop=True)
                s_pe.inc(e.matmul(p_b1[:, :], C.ones_row_f[0:1, :],
                                  r_row[0:1, 512:1024], start=True, stop=True))

            @blk.vector
            def _(e):
                s_pe.wait(e)
                s_ve.inc(e.tensor_copy(out=rB[:, 0:512], in_=p_b0[:, :]))
                s_ve.inc(e.tensor_copy(out=rB[:, 512:1024], in_=p_b1[:, :]))

        _ps.close()

        def norm_ops(e, ks, sem, half, buf=None):
            b = tmp_n if buf is None else buf
            for k in ks:
                sem.inc(e.tensor_sub(b[:, half * 1024:half * 1024 + 1024],
                                     xT[:, k * T:(k + 1) * T], mB[:, :]))
                sem.wait(e)
                sem.inc(e.tensor_mul(xlnT[:, k * T:(k + 1) * T],
                                     b[:, half * 1024:half * 1024 + 1024],
                                     rB[:, :]))
                sem.wait(e)

        if gate is None:
            with nc.Block(_u("ln_norm")) as blk:
                s_n = Sem(nc, "ln_n")
                s_n2 = Sem(nc, "ln_n2")

                @blk.vector
                def _(e):
                    norm_ops(e, range(0, 10), s_n, 0)

                @blk.gpsimd
                def _(e):
                    norm_ops(e, range(10, NK), s_n2, 1)
            return

        wg_sb, ind, indtv = gate["wg_sb"], gate["ind"], gate["indtv"]
        with (
            nc.psum_tensor(_u("p_ws"), [1, 16], F32) as p_ws,
            nc.psum_tensor(_u("p_wb"), [128, 16], F32) as p_wb,
            nc.psum_tensor(_u("p_gl"), [128, 8 * 16], F32) as p_gl,
            nc.sbuf_tensor(_u("g_sc"), [128, 512], F32) as g_sc,
            nc.sbuf_tensor(_u("gn_sc"), [128, 1024], F32) as gn_sc,
            nc.sbuf_tensor(_u("ws_row"), [1, 16], F32) as ws_row,
            nc.sbuf_tensor(_u("wsB"), [128, 16], F32) as wsB,
        ):
            raw = g_sc[:, 0:128]
            lgs = g_sc[:, 128:256]
            nlmax = g_sc[:, 256:264]
            exs = g_sc[:, 272:400]
            ssum = g_sc[:, 400:408]
            tvc = g_sc[:, 408:416]
            t2 = g_sc[:, 432:448]
            nmcol = g_sc[:, 448:456]
            s_g1 = Sem(nc, "g1")
            s_g2 = Sem(nc, "g2")
            s_ga = Sem(nc, "ga")
            s_gb = Sem(nc, "gb")
            s_gc = Sem(nc, "gc")
            s_n2 = Sem(nc, "ln_n2")
            with nc.Block(_u("norm_gate")) as blk:
                @blk.gpsimd
                def _(e):
                    norm_ops(e, range(0, 12), s_n2, 0)

                @blk.tensor
                def _(e):
                    ins = None
                    for k in range(NK):
                        ins = e.matmul(p_ws[0:1, :], C.ones_col_f[:, :],
                                       wg_sb[:, 16 * k:16 * k + 16],
                                       start=(k == 0), stop=(k == NK - 1))
                    s_g1.inc(ins)
                    ins = None
                    for tc in range(TC):
                        for k in range(NK):
                            ins = e.matmul(
                                p_gl[:, 16 * tc:16 * tc + 16],
                                xT[:, k * T + 128 * tc:k * T + 128 * tc + 128],
                                wg_sb[:, 16 * k:16 * k + 16],
                                start=(k == 0), stop=(k == NK - 1))
                    s_ga.inc(ins)

                @blk.vector
                def _(e):
                    s_g1.wait(e, 1)
                    s_g2.inc(e.tensor_copy(out=ws_row[:, :], in_=p_ws[0:1, :]))

                @blk.tensor
                def _(e):
                    s_g2.wait(e, 1)
                    s_g1.inc(e.matmul(p_wb[:, :], C.ones_row_f[0:1, :],
                                      ws_row[0:1, :], start=True, stop=True))

                @blk.vector
                def _(e):
                    s_g1.wait(e, 2)
                    s_g2.inc(e.tensor_copy(out=wsB[:, :], in_=p_wb[:, :]))
                    s_g2.inc(e.tensor_scalar_mul(nmcol, mrcol[:, 0:TC], -1.0))
                    s_g2.wait(e)
                    s_ga.wait(e)
                    for tc in range(TC):
                        sl = slice(16 * tc, 16 * tc + 16)
                        s_gb.inc(e.scalar_tensor_tensor(
                            t2, wsB[:, :], nmcol[:, tc:tc + 1], p_gl[:, sl],
                            ALU.mult, ALU.add))
                        s_gb.wait(e)
                        s_gb.inc(e.tensor_scalar(lgs[:, sl], t2,
                                                 mrcol[:, TC + tc:TC + tc + 1],
                                                 None, ALU.mult))
                        s_gb.wait(e)
                        s_gb.inc(e.reduce_max(nlmax[:, tc:tc + 1], lgs[:, sl],
                                              axis=AX.X, negate=True))
                        s_gb.wait(e)
                        s_gb.inc(e.tensor_single_scalar(
                            ind[:, sl], lgs[:, sl], nlmax[:, tc:tc + 1],
                            ALU.add))
                        s_gb.wait(e)
                        s_gb.inc(e.tensor_single_scalar(
                            ind[:, sl], ind[:, sl], 0.0, ALU.is_ge))
                        s_gb.wait(e)

                @blk.scalar
                def _(e):
                    s_gb.wait(e)
                    ins = None
                    for tc in range(TC):
                        sl = slice(16 * tc, 16 * tc + 16)
                        ins = e.activation(out=exs[:, sl], in_=lgs[:, sl],
                                           func=AF.Exp,
                                           bias=nlmax[:, tc:tc + 1])
                    s_gc.inc(ins)

                @blk.vector
                def _(e):
                    s_gc.wait(e)
                    for tc in range(TC):
                        sl = slice(16 * tc, 16 * tc + 16)
                        s_gb.inc(e.reduce_sum(ssum[:, tc:tc + 1], exs[:, sl],
                                              axis=AX.X))
                        s_gb.wait(e)
                        s_gb.inc(e.reciprocal(tvc[:, tc:tc + 1],
                                              ssum[:, tc:tc + 1]))
                        s_gb.wait(e)
                        s_gb.inc(e.tensor_scalar(indtv[:, sl], ind[:, sl],
                                                 tvc[:, tc:tc + 1], None,
                                                 ALU.mult))
                        s_gb.wait(e)
                    s_ngd = Sem(nc, "ngd")
                    norm_ops(e, range(12, NK), s_ngd, 0, buf=gn_sc)



def emit_proj_feat(nc, C, w_dram, n_oc, srcT, out_dram, out_f32_add=None,
                   out_dram_f32=None):
    wbf = w_dram.dtype == BF16
    with (
        nc.psum_tensor(_u("pp0"), [128, 1024], F32) as pp0,
        nc.psum_tensor(_u("pp1"), [128, 1024], F32) as pp1,
        nc.sbuf_tensor(_u("wf"), [128, 2 * NK * 128],
                       BF16 if wbf else F32) as wf,
        nc.sbuf_tensor(_u("wb"), [128, 3 * NK * 128], BF16) as wb,
        nc.sbuf_tensor(_u("stg"), [128, 2 * 1024],
                       BF16 if out_f32_add is None else F32) as stg,
    ):
        if wbf:
            wb = wf
        s_wd = DSem(nc, "pf_wd")
        s_wc = Sem(nc, "pf_wc")
        s_pe = Sem(nc, "pf_pe")
        s_ev = Sem(nc, "pf_ev")
        s_st = DSem(nc, "pf_st")
        s_wb = Sem(nc, "pf_wb")
        pp = [pp0, pp1]
        with nc.Block(_u("proj_feat")) as blk:
            @blk.sync
            def _(e):
                for oc in range(n_oc):
                    par = oc % 2
                    if oc >= 2:
                        if wbf:
                            s_pe.wait(e, oc - 1)
                        else:
                            s_wc.wait(e, oc - 1)
                    s_wd.inc(oc, e.dma_start(
                        out=wf[:, par * NK * 128:(par + 1) * NK * 128]
                        .rearrange("p (k o) -> p k o", o=128),
                        in_=w_dram[:, 128 * oc:128 * oc + 128]
                        .rearrange("(k p) o -> p k o", p=128)))

            @blk.gpsimd
            def _(e):
                if wbf:
                    return
                for oc in range(n_oc):
                    par = oc % 2
                    par3 = oc % 3
                    s_wd.wait_done(e, oc)
                    if oc >= 3:
                        s_pe.wait(e, oc - 2)
                    s_wc.inc(e.tensor_copy(
                        out=wb[:, par3 * NK * 128:(par3 + 1) * NK * 128],
                        in_=wf[:, par * NK * 128:(par + 1) * NK * 128]))

            @blk.tensor
            def _(e):
                for oc in range(n_oc):
                    par = oc % 2
                    if wbf:
                        s_wd.wait_done(e, oc)
                        wbase = par * NK * 128
                    else:
                        s_wc.wait(e, oc + 1)
                        wbase = (oc % 3) * NK * 128
                    if oc >= 2:
                        s_ev.wait(e, oc - 1)
                    ins = None
                    for k in range(NK):
                        st, sp = (k == 0), (k == NK - 1)
                        base = wbase + 128 * k
                        e.matmul(pp[par][:, 0:512], wb[:, base:base + 128],
                                 srcT[:, k * T:k * T + 512], start=st, stop=sp)
                        ins = e.matmul(pp[par][:, 512:1024],
                                       wb[:, base:base + 128],
                                       srcT[:, k * T + 512:k * T + 1024],
                                       start=st, stop=sp)
                    s_pe.inc(ins)

            @blk.vector
            def _(e):
                for oc in range(n_oc):
                    par = oc % 2
                    s_pe.wait(e, oc + 1)
                    if oc >= 2:
                        s_st.wait_done(e, oc - 2)
                        if out_f32_add is not None:
                            s_wb.wait(e, oc - 1)
                    if out_f32_add is None:
                        s_ev.inc(e.tensor_copy(
                            out=stg[:, par * 1024:(par + 1) * 1024],
                            in_=pp[par][:, :]))
                    else:
                        s_ev.inc(e.tensor_add(
                            stg[:, par * 1024:(par + 1) * 1024],
                            pp[par][:, :],
                            out_f32_add[:, oc * T:(oc + 1) * T]))

            @blk.scalar
            def _(e):
                dram = out_dram if out_dram_f32 is None else out_dram_f32
                for oc in range(n_oc):
                    par = oc % 2
                    s_ev.wait(e, oc + 1)
                    if out_f32_add is not None:
                        s_wb.inc(e.copy(
                            out=out_f32_add[:, oc * T:(oc + 1) * T],
                            in_=stg[:, par * 1024:(par + 1) * 1024]))
                    s_st.inc(oc, e.dma_start(
                        out=dram[128 * oc:128 * oc + 128, :],
                        in_=stg[:, par * 1024:(par + 1) * 1024]))
                s_st.wait_all(e)


def emit_proj_tok(nc, C, w_dram, n_cg, srcT, consume):
    wbf = w_dram.dtype == BF16
    with (
        nc.psum_tensor(_u("pt0"), [128, 512], F32) as b0,
        nc.psum_tensor(_u("pt1"), [128, 512], F32) as b1,
        nc.psum_tensor(_u("pt2"), [128, 512], F32) as b2,
        nc.psum_tensor(_u("pt3"), [128, 512], F32) as b3,
        nc.psum_tensor(_u("pt4"), [128, 512], F32) as b4,
        nc.psum_tensor(_u("pt5"), [128, 512], F32) as b5,
        nc.psum_tensor(_u("pt6"), [128, 512], F32) as b6,
        nc.psum_tensor(_u("pt7"), [128, 512], F32) as b7,
        nc.sbuf_tensor(_u("wtf"), [128, 2 * 8 * 512],
                       BF16 if wbf else F32) as wtf,
        nc.sbuf_tensor(_u("wtb"), [128, 4 * 512], BF16) as wtb,
    ):
        banks = [b0, b1, b2, b3, b4, b5, b6, b7]
        s_wd = DSem(nc, "pt_wd")
        s_wc = Sem(nc, "pt_wc")
        s_pe = Sem(nc, "pt_pe")
        s_ev = Sem(nc, "pt_ev")
        KG = NK // 8
        with nc.Block(_u("proj_tok")) as blk:
            @blk.sync
            def _(e):
                for cg in range(n_cg):
                    for kg in range(KG):
                        i2 = cg * KG + kg
                        par = i2 % 2
                        if i2 >= 2:
                            if wbf:
                                s_pe.wait(e, 8 * (i2 - 1))
                            else:
                                s_wc.wait(e, 8 * (i2 - 1))
                        s_wd.inc(i2, e.dma_start(
                            out=wtf[:, par * 8 * 512:(par + 1) * 8 * 512]
                            .rearrange("p (k c) -> p k c", c=512),
                            in_=w_dram[1024 * kg:1024 * kg + 1024,
                                       512 * cg:512 * cg + 512]
                            .rearrange("(k p) c -> p k c", p=128)))

            @blk.scalar
            def _(e):
                if wbf:
                    return
                for cg in range(n_cg):
                    for k in range(NK):
                        i = cg * NK + k
                        par = i % 4
                        i2 = cg * KG + k // 8
                        s_wd.wait_done(e, i2)
                        if i >= 4:
                            s_pe.wait(e, i - 3)
                        s_wc.inc(e.copy(
                            out=wtb[:, par * 512:(par + 1) * 512],
                            in_=wtf[:, (i2 % 2) * 8 * 512 + (k % 8) * 512:
                                    (i2 % 2) * 8 * 512 + (k % 8) * 512 + 512]))

            @blk.tensor
            def _(e):
                for cg in range(n_cg):
                    for k in range(NK):
                        i = cg * NK + k
                        par = i % 2
                        i2 = cg * KG + k // 8
                        if wbf:
                            s_wd.wait_done(e, i2)
                            wop = wtf[:, (i2 % 2) * 8 * 512 + (k % 8) * 512:
                                      (i2 % 2) * 8 * 512 + (k % 8) * 512 + 512]
                        else:
                            s_wc.wait(e, i + 1)
                            wop = wtb[:, (i % 4) * 512:(i % 4) * 512 + 512]
                        if k == 0 and cg >= 1:
                            s_ev.wait(e, 8 * cg)
                        ins = None
                        for tc in range(TC):
                            ins = e.matmul(
                                banks[tc][:, :],
                                srcT[:, k * T + 128 * tc:k * T + 128 * tc + 128],
                                wop,
                                start=(k == 0), stop=(k == NK - 1))
                        s_pe.inc(ins)

            @blk.vector
            def _(e):
                for cg in range(n_cg):
                    s_pe.wait(e, NK * (cg + 1))
                    consume(e, cg, banks, s_ev)




def proj_tok_res_alloc(nc):
    return nc.alloc_sbuf_tensor(_u("wres"), [128, 2 * NK * 1024], BF16)


def proj_tok_res_prefetch(nc, w_dram, wres, s_wd):
    n_pass = w_dram.shape[1] // 1024
    with nc.Block(_u("wpref")) as blk:
        @blk.sync
        def _(e):
            for p in range(min(2, n_pass)):
                s_wd.inc(p, e.dma_start(
                    out=wres[:, (p % 2) * NK * 1024:(p % 2 + 1) * NK * 1024]
                    .rearrange("p (k c) -> p k c", c=1024),
                    in_=w_dram[:, 1024 * p:1024 * p + 1024]
                    .rearrange("(k p) c -> p k c", p=128)))


def emit_proj_tok_res(nc, C, w_dram, srcT, consume_tc, wres, s_wd,
                      dma_out=None):
    n_pass = w_dram.shape[1] // 1024
    with (
        nc.psum_tensor(_u("pr0"), [128, 512], F32) as b0,
        nc.psum_tensor(_u("pr1"), [128, 512], F32) as b1,
        nc.psum_tensor(_u("pr2"), [128, 512], F32) as b2,
        nc.psum_tensor(_u("pr3"), [128, 512], F32) as b3,
        nc.psum_tensor(_u("pr4"), [128, 512], F32) as b4,
        nc.psum_tensor(_u("pr5"), [128, 512], F32) as b5,
        nc.psum_tensor(_u("pr6"), [128, 512], F32) as b6,
        nc.psum_tensor(_u("pr7"), [128, 512], F32) as b7,
    ):
        banks = [b0, b1, b2, b3, b4, b5, b6, b7]
        s_pe = Sem(nc, "pr_pe")
        s_ev = Sem(nc, "pr_ev")
        with nc.Block(_u("proj_res")) as blk:
            @blk.sync
            def _(e):
                for p in range(2, n_pass):
                    s_pe.wait(e, 8 * (p - 1))
                    s_wd.inc(p, e.dma_start(
                        out=wres[:, (p % 2) * NK * 1024:(p % 2 + 1) * NK * 1024]
                        .rearrange("p (k c) -> p k c", c=1024),
                        in_=w_dram[:, 1024 * p:1024 * p + 1024]
                        .rearrange("(k p) c -> p k c", p=128)))

            @blk.tensor
            def _(e):
                for p in range(n_pass):
                    par = p % 2
                    s_wd.wait_done(e, p)
                    for tc in range(TC):
                        idx = p * TC + tc
                        if idx >= 4:
                            s_ev.wait(e, 2 * (idx - 3))
                        bb = [banks[(tc % 4) * 2], banks[(tc % 4) * 2 + 1]]
                        ins = None
                        for k in range(NK):
                            wof = par * NK * 1024 + k * 1024
                            for cg2 in range(2):
                                ins = e.matmul(
                                    bb[cg2][:, :],
                                    srcT[:, k * T + 128 * tc:
                                         k * T + 128 * tc + 128],
                                    wres[:, wof + 512 * cg2:wof + 512 * cg2 + 512],
                                    start=(k == 0), stop=(k == NK - 1))
                        s_pe.inc(ins)

            @blk.vector
            def _(e):
                for p in range(n_pass):
                    for tc in range(TC):
                        idx = p * TC + tc
                        s_pe.wait(e, idx + 1)
                        bb = [banks[(tc % 4) * 2], banks[(tc % 4) * 2 + 1]]
                        consume_tc(e, p, tc, bb, s_ev)

            if dma_out is not None:
                stage, out_dram, s_st = dma_out

                @blk.scalar
                def _(e):
                    for p in range(n_pass):
                        for tc in range(TC):
                            for cg2 in range(2):
                                i = (p * TC + tc) * 2 + cg2
                                s_ev.wait(e, i + 1)
                                s_st.inc(i, e.dma_start(
                                    out=out_dram[
                                        128 * tc:128 * tc + 128,
                                        1024 * p + 512 * cg2:
                                        1024 * p + 512 * cg2 + 512],
                                    in_=stage[:, 512 * (i % 2):
                                              512 * (i % 2) + 512]))
                    s_st.wait_all(e)


def selection_consume_tc(kacc, vacc, ind, indtv, s_chain):
    def consume(e, p, tc, bb, s_ev):
        for cg2 in range(2):
            colbase = 1024 * p + 512 * cg2
            kpart = colbase < D
            acc = kacc if kpart else vacc
            wsel = ind if kpart else indtv
            for hl in range(4):
                h = ((colbase % D) // 128) + hl
                first = (colbase % D == 0 and hl == 0)
                a = acc[:, 128 * tc:128 * tc + 128]
                pp = bb[cg2][:, 128 * hl:128 * hl + 128]
                wcol = wsel[:, 16 * tc + h:16 * tc + h + 1]
                if first:
                    ins = e.tensor_scalar(a, pp, wcol, None, ALU.mult)
                else:
                    ins = e.scalar_tensor_tensor(a, pp, wcol, a,
                                                 ALU.mult, ALU.add)
                s_chain.inc(ins)
                s_chain.wait(e)
            s_ev.inc(e.engine_nop())
    return consume


def evac_stage_consume_tc(stage, s_st):
    ctr = [0]

    def consume(e, p, tc, bb, s_ev):
        for cg2 in range(2):
            i = ctr[0]
            ctr[0] += 1
            slot = i % 2
            s_st.wait_done(e, i - 2)
            s_ev.inc(e.tensor_copy(
                out=stage[:, 512 * slot:512 * slot + 512], in_=bb[cg2][:, :]))
    return consume


def evac_consume_tc(out_sb, width):
    def consume(e, p, tc, bb, s_ev):
        for cg2 in range(2):
            col = 1024 * p + 512 * cg2
            s_ev.inc(e.tensor_copy(
                out=out_sb[:, tc * width + col:tc * width + col + 512],
                in_=bb[cg2][:, :]))
    return consume


def selection_consume(kacc, vacc, ind, indtv, s_chain):
    def consume(e, cg, banks, s_ev):
        kpart = cg < 4
        acc = kacc if kpart else vacc
        wsel = ind if kpart else indtv
        for hl in range(4):
            for tc in range(TC):
                h = 4 * (cg % 4) + hl
                first = (cg % 4 == 0 and hl == 0)
                a = acc[:, 128 * tc:128 * tc + 128]
                p = banks[tc][:, 128 * hl:128 * hl + 128]
                wcol = wsel[:, 16 * tc + h:16 * tc + h + 1]
                if not first and s_chain.v >= TC:
                    s_chain.wait(e, s_chain.v - (TC - 1))
                if first:
                    ins = e.tensor_scalar(a, p, wcol, None, ALU.mult)
                else:
                    ins = e.scalar_tensor_tensor(a, p, wcol, a,
                                                 ALU.mult, ALU.add)
                s_chain.inc(ins)
                if hl == 3:
                    s_ev.inc(e.engine_nop())
    return consume


def evac_consume(out_sb, width):
    def consume(e, cg, banks, s_ev):
        for tc in range(TC):
            s_ev.inc(e.tensor_copy(
                out=out_sb[:, tc * width + 512 * cg:tc * width + 512 * cg + 512],
                in_=banks[tc][:, :]))
    return consume



def emit_flash2(nc, C, qhT, khT, kh_stride, vt, vt_stride, vt_hoff, masks,
                scale_post, fy_dtype, y_out_fn, add_src=None,
                side_section=None):
    with (
        nc.psum_tensor(_u("fS0"), [128, 1024], F32) as S0,
        nc.psum_tensor(_u("fS1"), [128, 1024], F32) as S1,
        nc.psum_tensor(_u("fY0"), [128, 512], F32) as Y0,
        nc.psum_tensor(_u("fY1"), [128, 512], F32) as Y1,
        nc.psum_tensor(_u("fL0"), [128, 512], F32) as L0,
        nc.psum_tensor(_u("fL1"), [128, 512], F32) as L1,
        nc.sbuf_tensor(_u("fpT"), [128, 4 * 1024], BF16) as pT,
        nc.sbuf_tensor(_u("frl"), [1, 2 * 512], F32) as rl,
        nc.sbuf_tensor(_u("frb"), [128, 2 * 512], F32) as rb,
        nc.sbuf_tensor(_u("ffy"), [128, 2 * 512], fy_dtype) as fy,
    ):
        Sp = [S0, S1]
        Yp = [Y0, Y1]
        Lp = [L0, L1]
        s_S = Sem(nc, "f_S")
        s_E = Sem(nc, "f_E")
        s_M = Sem(nc, "f_M")
        s_PV = Sem(nc, "f_PV")
        s_R = Sem(nc, "f_R")
        s_B = Sem(nc, "f_B")
        s_YE = Sem(nc, "f_YE")
        s_Y = DSem(nc, "f_Y")

        groups = []
        qblocks = []
        gidx = 0
        qidx = 0
        for hl in range(HPC):
            for qb in range(NQB):
                ng = 2 * qb + 2
                qblocks.append((hl, qb, ng, gidx, qidx))
                for g in range(ng):
                    groups.append((hl, qb, g, g >= ng - 2, gidx))
                    gidx += 1
                qidx += 1
        nmask_upto = {}
        acc = 0
        for (hl, qb, g, masked, gi) in groups:
            if masked:
                acc += 1
            nmask_upto[gi] = acc
        g2q = {}
        for (hl, qb, ng, g0, qi) in qblocks:
            for g in range(ng):
                g2q[g0 + g] = qi

        with nc.Block(_u("flash")) as blk:
            if side_section is not None:
                side_section(blk)

            @blk.tensor
            def _(e):
                def emit_S(gi):
                    (hl, qb, g, masked, _) = groups[gi]
                    sp = gi % 2
                    qs = hl * S + 512 * qb
                    ins = None
                    for half in range(2):
                        kc = 2 * g + half
                        kb = hl * kh_stride + 128 * kc
                        ins = e.matmul(Sp[sp][:, 512 * half:512 * half + 512],
                                       khT[:, kb:kb + 128],
                                       qhT[:, qs:qs + 512],
                                       start=True, stop=True)
                    s_S.inc(ins)

                def emit_Y(gi):
                    (hl, qb, g, masked, _) = groups[gi]
                    sp = gi % 4
                    qi = g2q[gi]
                    yb = qi % 2
                    ng = 2 * qb + 2
                    pos = g
                    if pos == 0 and qi >= 2:
                        s_YE.wait(e, 2 * (qi - 1))
                    for half in range(2):
                        kc = 2 * g + half
                        vb = kc * vt_stride + vt_hoff * hl
                        e.matmul(Yp[yb][:, :], vt[:, vb:vb + 128],
                                 pT[:, sp * 1024 + 512 * half:
                                    sp * 1024 + 512 * half + 512],
                                 start=(pos == 0 and half == 0),
                                 stop=(pos == ng - 1 and half == 1))

                def emit_L(gi):
                    (hl, qb, g, masked, _) = groups[gi]
                    sp = gi % 4
                    qi = g2q[gi]
                    yb = qi % 2
                    ng = 2 * qb + 2
                    pos = g
                    ins = None
                    for half in range(2):
                        ins = e.matmul(Lp[yb][0:1, :], C.ones_col_b[:, :],
                                       pT[:, sp * 1024 + 512 * half:
                                          sp * 1024 + 512 * half + 512],
                                       start=(pos == 0 and half == 0),
                                       stop=(pos == ng - 1 and half == 1))
                    if pos == ng - 1:
                        s_PV.inc(ins)

                def emit_BC(qi):
                    yb = qi % 2
                    s_R.wait(e, qi + 1)
                    s_B.inc(e.matmul(Lp[yb][:, :], C.ones_row_f[0:1, :],
                                     rl[0:1, 512 * yb:512 * yb + 512],
                                     start=True, stop=True))

                NG = len(groups)
                NB = NG // 2
                pending_bc = []
                def emit_S_batch(b):
                    if b >= NB:
                        return
                    for gi in (2 * b, 2 * b + 1):
                        if gi >= 2:
                            s_E.wait(e, gi - 1)
                        emit_S(gi)

                def emit_PV_batch(b):
                    if b == NB - 1:
                        s_E.wait(e, 2 * NB)
                    mneed = 0
                    for gi in (2 * b, 2 * b + 1):
                        if groups[gi][3]:
                            mneed = max(mneed, nmask_upto[gi])
                    if mneed:
                        s_M.wait(e, mneed)
                    for gi in (2 * b, 2 * b + 1):
                        emit_Y(gi)
                    for gi in (2 * b, 2 * b + 1):
                        emit_L(gi)

                emit_S_batch(0)
                for b in range(NB):
                    emit_S_batch(b + 1)
                    while pending_bc:
                        emit_BC(pending_bc.pop(0))
                    emit_PV_batch(b)
                    if groups[2 * b + 1][2] == 2 * groups[2 * b + 1][1] + 1:
                        pending_bc.append(g2q[2 * b + 1])
                while pending_bc:
                    emit_BC(pending_bc.pop(0))

            @blk.scalar
            def _(e):
                for (hl, qb, g, masked, gi) in groups:
                    sp4 = gi % 4
                    s_S.wait(e, gi + 1)
                    s_E.inc(e.activation(out=pT[:, sp4 * 1024:sp4 * 1024 + 1024],
                                         in_=Sp[gi % 2][:, :], func=AF.Exp,
                                         bias=C.bias_sh[:, 0:1],
                                         scale=SM_SCALE))

            @blk.vector
            def _(e):
                s_rl = Sem(nc, "f_rl")
                for (hl, qb, ng, g0, qi) in qblocks:
                    yb = qi % 2
                    for gi in (g0 + ng - 2, g0 + ng - 1):
                        g = groups[gi][2]
                        sp = gi % 4
                        s_E.wait(e, gi + 1)
                        mask = masks[0] if g == ng - 2 else masks[1]
                        s_M.inc(e.tensor_mul(
                            pT[:, sp * 1024:sp * 1024 + 1024],
                            pT[:, sp * 1024:sp * 1024 + 1024], mask[:, :]))
                    s_PV.wait(e, qi + 1)
                    s_rl.inc(e.reciprocal(rl[0:1, 512 * yb:512 * yb + 512],
                                          Lp[yb][0:1, :]))
                    s_rl.wait(e)
                    if scale_post != 1.0:
                        s_rl.inc(e.tensor_scalar_mul(
                            rl[0:1, 512 * yb:512 * yb + 512],
                            rl[0:1, 512 * yb:512 * yb + 512],
                            float(scale_post)))
                        s_rl.wait(e)
                    s_R.inc(e.engine_nop())
                    s_B.wait(e, qi + 1)
                    s_Y.wait_done(e, qi - 2)
                    s_YE.inc(e.tensor_copy(out=rb[:, 512 * yb:512 * yb + 512],
                                           in_=Lp[yb][:, :]))
                    s_YE.wait(e)
                    if add_src is None:
                        s_YE.inc(e.tensor_mul(fy[:, 512 * yb:512 * yb + 512],
                                              Yp[yb][:, :],
                                              rb[:, 512 * yb:512 * yb + 512]))
                    else:
                        s_rl.inc(e.tensor_mul(rb[:, 512 * yb:512 * yb + 512],
                                              Yp[yb][:, :],
                                              rb[:, 512 * yb:512 * yb + 512]))
                        s_rl.wait(e)
                        s_YE.inc(e.tensor_add(
                            fy[:, 512 * yb:512 * yb + 512],
                            add_src[:, hl * S + 512 * qb:hl * S + 512 * qb + 512],
                            rb[:, 512 * yb:512 * yb + 512]))

            @blk.sync
            def _(e):
                for (hl, qb, ng, g0, qi) in qblocks:
                    yb = qi % 2
                    s_YE.wait(e, 2 * (qi + 1))
                    y_out_fn(e, hl, qb, qi, fy[:, 512 * yb:512 * yb + 512], s_Y)
                s_Y.wait_all(e)


def emit_transpose_seq(nc, C, src, dst, n, ident):
    with (
        nc.psum_tensor(_u("ptr0"), [128, 128], BF16) as q0,
        nc.psum_tensor(_u("ptr1"), [128, 128], BF16) as q1,
    ):
        qp = [q0, q1]
        s_t = Sem(nc, "tr_t")
        s_c = Sem(nc, "tr_c")
        with nc.Block(_u("transpose")) as blk:
            @blk.tensor
            def _(e):
                for j in range(n):
                    par = j % 2
                    if j >= 2:
                        s_c.wait(e, j - 1)
                    s_t.inc(e.transpose(qp[par][:, :],
                                        src[:, 128 * j:128 * j + 128],
                                        ident[:, :]))

            @blk.vector
            def _(e):
                for j in range(n):
                    par = j % 2
                    s_t.wait(e, j + 1)
                    s_c.inc(e.tensor_copy(out=dst[:, 128 * j:128 * j + 128],
                                          in_=qp[par][:, :]))




def make_side_cast(nc, pairs, chunk_cols=2048):
    scf = nc.alloc_sbuf_tensor(_u("scf"), [128, 2 * chunk_cols], F32)
    scb = nc.alloc_sbuf_tensor(_u("scb"), [128, 2 * chunk_cols], BF16)
    s_d = DSem(nc, "sc_d")
    s_c = Sem(nc, "sc_c")
    s_o = DSem(nc, "sc_o")
    jobs = []
    for (srcp, dstp) in pairs:
        R, Cc = srcp.shape
        assert Cc == chunk_cols and R % 128 == 0
        for r in range(R // 128):
            jobs.append((srcp, dstp, r))

    def section(blk):
        @blk.gpsimd
        def _(e):
            def load(i):
                if i >= len(jobs):
                    return
                srcp, dstp, r = jobs[i]
                s_d.inc(i, e.dma_start(
                    out=scf[:, (i % 2) * chunk_cols:(i % 2 + 1) * chunk_cols],
                    in_=srcp[128 * r:128 * r + 128, :]))
            load(0)
            load(1)
            for i, (srcp, dstp, r) in enumerate(jobs):
                par = i % 2
                s_d.wait_done(e, i)
                s_o.wait_done(e, i - 2)
                s_c.inc(e.tensor_copy(
                    out=scb[:, par * chunk_cols:(par + 1) * chunk_cols],
                    in_=scf[:, par * chunk_cols:(par + 1) * chunk_cols]))
                s_c.wait(e)
                s_o.inc(i, e.dma_start(
                    out=dstp[128 * r:128 * r + 128, :],
                    in_=scb[:, par * chunk_cols:(par + 1) * chunk_cols]))
                load(i + 2)
            s_o.wait_all(e)
    return section



def build_prep1(with_carry=False):
    nc = bass.Bass("TRN2", target_bir_lowering=False)
    C = Consts(nc)
    x_t = nc.dram_tensor("x_t", [D, T], F32, kind="ExternalInput").ap()
    wg = nc.dram_tensor("wg", [D, 16], F32, kind="ExternalInput").ap()
    WDT = BF16 if with_carry else F32
    wkv = nc.dram_tensor("wkv", [D, 2 * D], WDT, kind="ExternalInput").ap()
    wq = nc.dram_tensor("wq", [D, D], WDT, kind="ExternalInput").ap()
    if with_carry:
        y2_t = nc.dram_tensor("y2_t", [D, T], BF16, kind="ExternalInput").ap()
        wo = nc.dram_tensor("wo", [D, D], BF16, kind="ExternalInput").ap()
        x2_t = nc.dram_tensor("x2_t", [D, T], F32, kind="ExternalOutput").ap()
    qt = nc.dram_tensor("qt", [D, T], BF16, kind="ExternalOutput").ap()
    ksht = nc.dram_tensor("ksht", [128, T], BF16, kind="ExternalOutput").ap()
    vsht = nc.dram_tensor("vsht", [T, 128], BF16, kind="ExternalOutput").ap()
    idn = nc.dram_tensor("idn", [128, 128], F32, kind="ExternalInput").ap()

    xT = nc.alloc_sbuf_tensor("xT", [128, NK * T], F32)
    xlnT = nc.alloc_sbuf_tensor("xlnT", [128, NK * T], BF16)
    mrcol = nc.alloc_sbuf_tensor("mrcol", [128, 16], F32)
    wg_sb = nc.alloc_sbuf_tensor("wg_sb", [128, NK * 16], F32)
    ind = nc.alloc_sbuf_tensor("ind_sb", [128, TC * 16], F32)
    indtv = nc.alloc_sbuf_tensor("indtv_sb", [128, TC * 16], F32)
    kacc = nc.alloc_sbuf_tensor("kacc", [128, TC * 128], F32)
    vacc = nc.alloc_sbuf_tensor("vacc", [128, TC * 128], F32)
    kvb = nc.alloc_sbuf_tensor("kvb", [128, TC * 128], BF16)
    ksb = nc.alloc_sbuf_tensor("ksb", [128, T], BF16)
    identb = nc.alloc_sbuf_tensor("identb", [128, 128], F32)

    C.emit(nc)
    loads = [
        (xT[:, :].rearrange("p (k t) -> p k t", t=T),
         x_t[:, :].rearrange("(k p) t -> p k t", p=128)),
        (wg_sb[:, :].rearrange("p (k h) -> p k h", h=16),
         wg[:, :].rearrange("(k p) h -> p k h", p=128)),
        (identb[:, :], idn[:, :]),
    ]
    if with_carry:
        loads.append((xlnT[:, :].rearrange("p (k t) -> p k t", t=T),
                      y2_t[:, :].rearrange("(k p) t -> p k t", p=128)))
        wres = proj_tok_res_alloc(nc)
        s_wkv = DSem(nc, "wkv")
    emit_loads(nc, loads)

    if with_carry:
        proj_tok_res_prefetch(nc, wkv, wres, s_wkv)
        emit_proj_feat(nc, C, wo, NK, xlnT, None, out_f32_add=xT,
                       out_dram_f32=x2_t)

    emit_ln(nc, C, xT, xlnT, mrcol,
            gate=dict(wg_sb=wg_sb, ind=ind, indtv=indtv))

    s_chain = Sem(nc, "selchain")
    if with_carry:
        emit_proj_tok_res(nc, C, wkv, xlnT,
                          selection_consume_tc(kacc, vacc, ind, indtv, s_chain),
                          wres, s_wkv)
    else:
        emit_proj_tok(nc, C, wkv, 8, xlnT,
                      selection_consume(kacc, vacc, ind, indtv, s_chain))

    s_kv = Sem(nc, "kv_cast")
    s_kvs = Sem(nc, "kv_st")
    s_ktr = Sem(nc, "kv_tr")
    with (
        nc.psum_tensor(_u("ptk0"), [128, 128], F32) as ptk0,
        nc.psum_tensor(_u("ptk1"), [128, 128], F32) as ptk1,
    ):
        ptk = [ptk0, ptk1]
        with nc.Block(_u("kv_out")) as blk:
            @blk.vector
            def _(e):
                for tc in range(TC):
                    s_kv.inc(e.tensor_copy(
                        out=kvb[:, 128 * tc:128 * tc + 128],
                        in_=vacc[:, 128 * tc:128 * tc + 128]))
                for tc in range(TC):
                    s_ktr.wait(e, tc + 1)
                    s_kv.inc(e.tensor_copy(
                        out=ksb[:, 128 * tc:128 * tc + 128],
                        in_=ptk[tc % 2][:, :]))

            @blk.tensor
            def _(e):
                for tc in range(TC):
                    if tc >= 2:
                        s_kv.wait(e, 8 + tc - 1)
                    s_ktr.inc(e.transpose(ptk[tc % 2][:, :],
                                          kacc[:, 128 * tc:128 * tc + 128],
                                          identb[:, :]))

            @blk.sync
            def _(e):
                s_kv.wait(e)
                s_kvs.inc(e.dma_start(
                    out=vsht[:, :].rearrange("(tc p) c -> p tc c", p=128),
                    in_=kvb[:, :].rearrange("p (tc c) -> p tc c", c=128)), 16)
                s_kvs.inc(e.dma_start(out=ksht[:, :], in_=ksb[:, :]), 16)
                s_kvs.wait(e)

    emit_proj_feat(nc, C, wq, NK, xlnT, qt)
    nc.finalize()
    return nc


def build_prep2():
    nc = bass.Bass("TRN2", target_bir_lowering=False)
    C = Consts(nc)
    x_t = nc.dram_tensor("x_t", [D, T], F32, kind="ExternalInput").ap()
    y1_t = nc.dram_tensor("y1_t", [D, T], F32, kind="ExternalInput").ap()
    wq = nc.dram_tensor("wq", [D, D], BF16, kind="ExternalInput").ap()
    wk = nc.dram_tensor("wk", [D, D], BF16, kind="ExternalInput").ap()
    wv = nc.dram_tensor("wv", [D, D], BF16, kind="ExternalInput").ap()
    x1_t = nc.dram_tensor("x1_t", [D, T], F32, kind="ExternalOutput").ap()
    q2t = nc.dram_tensor("q2t", [D, T], BF16, kind="ExternalOutput").ap()
    k2t = nc.dram_tensor("k2t", [D, T], BF16, kind="ExternalOutput").ap()
    v2t = nc.dram_tensor("v2t", [T, D], BF16, kind="ExternalOutput").ap()

    xT = nc.alloc_sbuf_tensor("xT", [128, NK * T], F32)
    xlnT = nc.alloc_sbuf_tensor("xlnT", [128, NK * T], BF16)
    mrcol = nc.alloc_sbuf_tensor("mrcol", [128, 16], F32)
    ybuf = nc.alloc_sbuf_tensor("ybuf", [128, 2 * T], F32)
    vstg = nc.alloc_sbuf_tensor("vstg", [128, 2 * 512], BF16)

    C.emit(nc)
    wres = proj_tok_res_alloc(nc)
    s_wv = DSem(nc, "wv")
    emit_loads(nc, [
        (xT[:, :].rearrange("p (k t) -> p k t", t=T),
         x_t[:, :].rearrange("(k p) t -> p k t", p=128)),
    ])
    proj_tok_res_prefetch(nc, wv, wres, s_wv)

    s_yd = DSem(nc, "x1_yd")
    s_ya = Sem(nc, "x1_ya")
    s_ys = DSem(nc, "x1_st")
    with nc.Block(_u("x1_add")) as blk:
        @blk.sync
        def _(e):
            for k in range(NK):
                par = k % 2
                if k >= 2:
                    s_ya.wait(e, k - 1)
                s_yd.inc(k, e.dma_start(
                    out=ybuf[:, par * T:(par + 1) * T],
                    in_=y1_t[128 * k:128 * k + 128, :]))

        @blk.vector
        def _(e):
            for k in range(NK):
                par = k % 2
                s_yd.wait_done(e, k)
                s_ya.inc(e.tensor_add(xT[:, k * T:(k + 1) * T],
                                      xT[:, k * T:(k + 1) * T],
                                      ybuf[:, par * T:(par + 1) * T]))
                s_ya.wait(e)

        @blk.sync
        def _(e):
            for k in range(NK):
                s_ya.wait(e, k + 1)
                s_ys.inc(k, e.dma_start(out=x1_t[128 * k:128 * k + 128, :],
                                        in_=xT[:, k * T:(k + 1) * T]))
            s_ys.wait_all(e)

    emit_ln(nc, C, xT, xlnT, mrcol, gate=None)
    emit_proj_feat(nc, C, wq, NK, xlnT, q2t)
    emit_proj_feat(nc, C, wk, NK, xlnT, k2t)
    s_v2st = DSem(nc, "v2st")
    emit_proj_tok_res(nc, C, wv, xlnT,
                      evac_stage_consume_tc(vstg, s_v2st), wres, s_wv,
                      dma_out=(vstg, v2t, s_v2st))
    nc.finalize()
    return nc


def build_flash_attn():
    nc = bass.Bass("TRN2", target_bir_lowering=False)
    C = Consts(nc)
    qh = nc.dram_tensor("qh", [HPC * 128, S], BF16, kind="ExternalInput").ap()
    kh = nc.dram_tensor("kh", [HPC * 128, S], BF16, kind="ExternalInput").ap()
    vh = nc.dram_tensor("vh", [S, HPC * 128], BF16, kind="ExternalInput").ap()
    msk = nc.dram_tensor("msk", [128, 2048], BF16, kind="ExternalInput").ap()
    yt = nc.dram_tensor("yt", [HPC * 128, S], BF16, kind="ExternalOutput").ap()
    wo_f = nc.dram_tensor("wo_f", [D // W, D], F32, kind="ExternalInput").ap()
    wo_b = nc.dram_tensor("wo_b", [D // W, D], BF16, kind="ExternalOutput").ap()

    qhT = nc.alloc_sbuf_tensor("qhT", [128, HPC * S], BF16)
    khT = nc.alloc_sbuf_tensor("khT", [128, HPC * S], BF16)
    vt = nc.alloc_sbuf_tensor("vt", [128, (S // 128) * 256], BF16)
    masks = nc.alloc_sbuf_tensor("masks", [128, 2048], BF16)

    C.emit(nc)
    side_section = make_side_cast(nc, [(wo_f, wo_b)])
    emit_loads(nc, [
        (qhT[:, :].rearrange("p (hl t) -> p hl t", t=S),
         qh[:, :].rearrange("(hl p) t -> p hl t", p=128)),
        (khT[:, :].rearrange("p (hl t) -> p hl t", t=S),
         kh[:, :].rearrange("(hl p) t -> p hl t", p=128)),
        (vt[:, :].rearrange("p (j c) -> p j c", c=256),
         vh[:, :].rearrange("(j p) c -> p j c", p=128)),
        (masks[:, :], msk[:, :]),
    ])

    def y_out(e, hl, qb, qi, src, s_Y):
        s_Y.inc(qi, e.dma_start(
            out=yt[128 * hl:128 * hl + 128, 512 * qb:512 * qb + 512],
            in_=src))

    emit_flash2(nc, C, qhT, khT, S, vt, 256, 128, (masks[:, 0:1024], masks[:, 1024:2048]),
                1.0, BF16, y_out, side_section=side_section)
    nc.finalize()
    return nc


def build_flash_ffn(final=False):
    nc = bass.Bass("TRN2", target_bir_lowering=False)
    C = Consts(nc)
    qh = nc.dram_tensor("qh", [HPC * 128, S], BF16, kind="ExternalInput").ap()
    khx = nc.dram_tensor("khx", [128, S], BF16, kind="ExternalInput").ap()
    vv = nc.dram_tensor("vv", [S, 128], BF16, kind="ExternalInput").ap()
    msk = nc.dram_tensor("msk", [128, 2048], BF16, kind="ExternalInput").ap()
    side = []
    if not final:
        R8 = D // W
        wq_f = nc.dram_tensor("wq_f", [R8, D], F32, kind="ExternalInput").ap()
        wk_f = nc.dram_tensor("wk_f", [R8, D], F32, kind="ExternalInput").ap()
        wv_f = nc.dram_tensor("wv_f", [R8, D], F32, kind="ExternalInput").ap()
        wkv_f = nc.dram_tensor("wkv_f", [R8, 2 * D], F32, kind="ExternalInput").ap()
        wq1_f = nc.dram_tensor("wq1_f", [R8, D], F32, kind="ExternalInput").ap()
        wq_b = nc.dram_tensor("wq_b", [R8, D], BF16, kind="ExternalOutput").ap()
        wk_b = nc.dram_tensor("wk_b", [R8, D], BF16, kind="ExternalOutput").ap()
        wv_b = nc.dram_tensor("wv_b", [R8, D], BF16, kind="ExternalOutput").ap()
        wkv_b = nc.dram_tensor("wkv_b", [R8, 2 * D], BF16, kind="ExternalOutput").ap()
        wq1_b = nc.dram_tensor("wq1_b", [R8, D], BF16, kind="ExternalOutput").ap()
        side = [(wq_f, wq_b), (wk_f, wk_b), (wv_f, wv_b),
                (wkv_f[:, 0:D], wkv_b[:, 0:D]),
                (wkv_f[:, D:2 * D], wkv_b[:, D:2 * D]),
                (wq1_f, wq1_b)]
    if final:
        x2h = nc.dram_tensor("x2h", [HPC * 128, S], F32, kind="ExternalInput").ap()
        x3h = nc.dram_tensor("x3h", [HPC * 128, S], F32, kind="ExternalOutput").ap()
    else:
        yt = nc.dram_tensor("yt", [HPC * 128, S], F32, kind="ExternalOutput").ap()

    qhT = nc.alloc_sbuf_tensor("qhT", [128, HPC * S], BF16)
    khT = nc.alloc_sbuf_tensor("khT", [128, S], BF16)
    vt = nc.alloc_sbuf_tensor("vt", [128, S], BF16)
    masks = nc.alloc_sbuf_tensor("masks", [128, 2048], BF16)

    C.emit(nc)
    side_section = make_side_cast(nc, side) if side else None
    if final:
        x2sb = nc.alloc_sbuf_tensor("x2sb", [128, HPC * S], F32)
    loads = [
        (khT[:, :], khx[:, :]),
        (qhT[:, :].rearrange("p (hl t) -> p hl t", t=S),
         qh[:, :].rearrange("(hl p) t -> p hl t", p=128)),
        (vt[:, :].rearrange("p (j c) -> p j c", c=128),
         vv[:, :].rearrange("(j p) c -> p j c", p=128)),
        (masks[:, :], msk[:, :]),
    ]
    if final:
        loads.append((x2sb[:, :].rearrange("p (hl t) -> p hl t", t=S),
                      x2h[:, :].rearrange("(hl p) t -> p hl t", p=128)))
    emit_loads(nc, loads)

    ydst = x3h if final else yt

    def y_out(e, hl, qb, qi, src, s_Y):
        s_Y.inc(qi, e.dma_start(
            out=ydst[128 * hl:128 * hl + 128, 512 * qb:512 * qb + 512],
            in_=src))

    emit_flash2(nc, C, qhT, khT, 0, vt, 128, 0,
                (masks[:, 0:1024], masks[:, 1024:2048]), 0.5, F32, y_out,
                add_src=(x2sb if final else None), side_section=side_section)

    nc.finalize()
    return nc



def _causal_masks():
    m = np.zeros((128, 2048), np.float32)
    for half, moff in ((0, 0), (1, 1), (2, 2), (3, 3)):
        j = np.arange(128)[:, None]
        i = np.arange(512)[None, :]
        m[:, 512 * half:512 * half + 512] = (i >= j + 128 * moff)
    return m.astype(BF)


def _run(nc, in_maps):
    res = run_bass_kernel_spmd(nc, in_maps, core_ids=list(range(W)))
    return res.results


_NC_CACHE = {}


def _get(name, builder, *a, **k):
    if name not in _NC_CACHE:
        _NC_CACHE[name] = builder(*a, **k)
    return _NC_CACHE[name]


def kernel(L0_attn_x, L0_attn_Wq, L0_attn_Wk, L0_attn_Wv, L0_attn_Wo,
           L0_ffn_W_gate, L0_ffn_W_qkv):
    x = np.ascontiguousarray(np.asarray(L0_attn_x, np.float32)[0])
    XT = np.ascontiguousarray(x.T)
    Wq = np.asarray(L0_attn_Wq, np.float32)
    Wk = np.asarray(L0_attn_Wk, np.float32)
    Wv = np.asarray(L0_attn_Wv, np.float32)
    Wo = np.asarray(L0_attn_Wo, np.float32)
    Wg = np.asarray(L0_ffn_W_gate, np.float32)
    Wqkv = np.asarray(L0_ffn_W_qkv, np.float32)

    wgT = np.ascontiguousarray(Wg.T)
    wq1T = np.ascontiguousarray(Wqkv[0:D].T)
    wkvT = np.ascontiguousarray(Wqkv[D:3 * D].T)
    wqT = np.ascontiguousarray(Wq.T)
    wkT = np.ascontiguousarray(Wk.T)
    wvT = np.ascontiguousarray(Wv.T)
    woT = np.ascontiguousarray(Wo.T)
    msk = _causal_masks()
    idn = np.eye(128, dtype=np.float32)

    def xslice(A, c):
        return np.ascontiguousarray(A[:, T * c:T * c + T])

    nc1 = _get("prep1", build_prep1, False)
    r1 = _run(nc1, [{"x_t": xslice(XT, c), "wg": wgT, "wkv": wkvT,
                     "wq": wq1T, "idn": idn} for c in range(W)])
    Q1 = np.concatenate([r1[c]["qt"] for c in range(W)], axis=1)
    KT1 = np.concatenate([r1[c]["ksht"] for c in range(W)], axis=1)
    VS1 = np.concatenate([r1[c]["vsht"] for c in range(W)], axis=0)

    nc2 = _get("ffn_flash", build_flash_ffn, False)
    R8 = D // W
    r2 = _run(nc2, [{"qh": np.ascontiguousarray(Q1[256 * c:256 * c + 256]),
                     "khx": KT1, "vv": VS1, "msk": msk,
                     "wq_f": np.ascontiguousarray(wqT[R8 * c:R8 * c + R8]),
                     "wk_f": np.ascontiguousarray(wkT[R8 * c:R8 * c + R8]),
                     "wv_f": np.ascontiguousarray(wvT[R8 * c:R8 * c + R8]),
                     "wkv_f": np.ascontiguousarray(wkvT[R8 * c:R8 * c + R8]),
                     "wq1_f": np.ascontiguousarray(wq1T[R8 * c:R8 * c + R8])}
                    for c in range(W)])
    Y1 = np.concatenate([r2[c]["yt"] for c in range(W)], axis=0)
    wqB = np.concatenate([r2[c]["wq_b"] for c in range(W)], axis=0)
    wkB = np.concatenate([r2[c]["wk_b"] for c in range(W)], axis=0)
    wvB = np.concatenate([r2[c]["wv_b"] for c in range(W)], axis=0)
    wkvB = np.concatenate([r2[c]["wkv_b"] for c in range(W)], axis=0)
    wq1B = np.concatenate([r2[c]["wq1_b"] for c in range(W)], axis=0)

    nc3 = _get("prep2", build_prep2)
    r3 = _run(nc3, [{"x_t": xslice(XT, c), "y1_t": xslice(Y1, c),
                     "wq": wqB, "wk": wkB, "wv": wvB} for c in range(W)])
    X1 = np.concatenate([r3[c]["x1_t"] for c in range(W)], axis=1)
    Q2 = np.concatenate([r3[c]["q2t"] for c in range(W)], axis=1)
    K2 = np.concatenate([r3[c]["k2t"] for c in range(W)], axis=1)
    V2 = np.concatenate([r3[c]["v2t"] for c in range(W)], axis=0)

    nc4 = _get("attn_flash", build_flash_attn)
    r4 = _run(nc4, [{"qh": np.ascontiguousarray(Q2[256 * c:256 * c + 256]),
                     "kh": np.ascontiguousarray(K2[256 * c:256 * c + 256]),
                     "vh": np.ascontiguousarray(V2[:, 256 * c:256 * c + 256]),
                     "msk": msk,
                     "wo_f": np.ascontiguousarray(woT[R8 * c:R8 * c + R8])}
                    for c in range(W)])
    Y2 = np.concatenate([r4[c]["yt"] for c in range(W)], axis=0)
    woB = np.concatenate([r4[c]["wo_b"] for c in range(W)], axis=0)

    nc5 = _get("prep3", build_prep1, True)
    r5 = _run(nc5, [{"x_t": xslice(X1, c), "y2_t": xslice(Y2, c),
                     "wo": woB, "wg": wgT, "wkv": wkvB, "wq": wq1B,
                     "idn": idn}
                    for c in range(W)])
    X2 = np.concatenate([r5[c]["x2_t"] for c in range(W)], axis=1)
    Q3 = np.concatenate([r5[c]["qt"] for c in range(W)], axis=1)
    KT3 = np.concatenate([r5[c]["ksht"] for c in range(W)], axis=1)
    VS3 = np.concatenate([r5[c]["vsht"] for c in range(W)], axis=0)

    nc6 = _get("ffn_flash_fin", build_flash_ffn, True)
    r6 = _run(nc6, [{"qh": np.ascontiguousarray(Q3[256 * c:256 * c + 256]),
                     "khx": KT3, "vv": VS3, "msk": msk,
                     "x2h": np.ascontiguousarray(X2[256 * c:256 * c + 256])}
                    for c in range(W)])
    X3 = np.concatenate([r6[c]["x3h"] for c in range(W)], axis=0)

    return np.ascontiguousarray(X3.T).reshape(1, S, D)


# version 17 (final)
# speedup vs baseline: 1.0252x; 1.0102x over previous
import numpy as np
import ml_dtypes
import concourse.bass as bass
import concourse.mybir as mybir
from concourse.bass_utils import run_bass_kernel_spmd

F32 = mybir.dt.float32
BF16 = mybir.dt.bfloat16
AF = mybir.ActivationFunctionType
ALU = mybir.AluOpType
AX = mybir.AxisListType
BF = ml_dtypes.bfloat16

S, D, H, HD, W = 8192, 2048, 16, 128, 8
T = S // W
HPC = H // W
NK = D // 128
TC = T // 128
NQB = S // 512
EPS = 1e-5
SM_SCALE = float(1.0 / np.sqrt(HD))
SHIFT = -20.0

_uid = [0]


def _u(name):
    _uid[0] += 1
    return f"{name}_{_uid[0]}"


class Sem:
    def __init__(self, nc, name):
        self.h = nc.alloc_semaphore(_u(name))
        self.v = 0

    def inc(self, instr, k=1):
        instr.then_inc(self.h, k)
        self.v += k
        return self.v

    def wait(self, eng, v=None):
        eng.wait_ge(self.h, self.v if v is None else v)


class DSem:

    def __init__(self, nc, name):
        self.s = [Sem(nc, name + "a"), Sem(nc, name + "b")]

    def inc(self, i, instr):
        return self.s[i % 2].inc(instr, 16)

    def wait_done(self, eng, i):
        if i < 0:
            return
        self.s[i % 2].wait(eng, 16 * (i // 2 + 1))

    def wait_all(self, eng):
        self.s[0].wait(eng)
        self.s[1].wait(eng)


class DSem4:

    def __init__(self, nc, name):
        self.s = [Sem(nc, name + str(i)) for i in range(4)]

    def inc(self, i, instr):
        return self.s[i % 4].inc(instr, 16)

    def wait_done(self, eng, i):
        if i < 0:
            return
        self.s[i % 4].wait(eng, 16 * (i // 4 + 1))

    def wait_all(self, eng):
        for s in self.s:
            s.wait(eng)


class Consts:
    def __init__(self, nc):
        self.ones_col_f = nc.alloc_sbuf_tensor(_u("c_ocf"), [128, 1], F32)
        self.ones_col_b = nc.alloc_sbuf_tensor(_u("c_ocb"), [128, 1], BF16)
        self.ones_row_f = nc.alloc_sbuf_tensor(_u("c_orf"), [1, 128], F32)
        self.one1_f = nc.alloc_sbuf_tensor(_u("c_o1f"), [1, 1], F32)
        self.bias_sh = nc.alloc_sbuf_tensor(_u("c_bsh"), [128, 1], F32)
        self.eps_t = nc.alloc_sbuf_tensor(_u("c_eps"), [1, 1], F32)

    def emit(self, nc):
        with nc.Block(_u("consts")) as blk:
            @blk.vector
            def _(e):
                e.memset(self.ones_col_f[:, :], 1.0)
                e.memset(self.ones_col_b[:, :], 1.0)
                e.memset(self.ones_row_f[:, :], 1.0)
                e.memset(self.one1_f[:, :], 1.0)
                e.memset(self.bias_sh[:, :], SHIFT)
                e.memset(self.eps_t[:, :], EPS)


def emit_loads(nc, loads, name="inputs"):
    s = Sem(nc, "s_in")
    with nc.Block(name) as blk:
        @blk.sync
        def _(e):
            for out_ap, in_ap in loads:
                s.inc(e.dma_start(out=out_ap, in_=in_ap), 16)
            s.wait(e)



def emit_ln(nc, C, xT, xlnT, mrcol, gate=None):
    import contextlib
    with (
        nc.sbuf_tensor(_u("mB"), [128, 1024], F32) as mB,
        nc.sbuf_tensor(_u("rB"), [128, 1024], F32) as rB,
        nc.sbuf_tensor(_u("tmp_n"), [128, (1 if gate else 2) * 1024],
                       F32) as tmp_n,
    ):
        _ps = contextlib.ExitStack()
        sq_sc = _ps.enter_context(
            nc.sbuf_tensor(_u("sq_sc"), [128, 2 * 1024], BF16))
        xb_sc = _ps.enter_context(
            nc.sbuf_tensor(_u("xb_sc"), [128, 2 * 1024], BF16))
        rows = _ps.enter_context(
            nc.sbuf_tensor(_u("rows"), [1, 3 * 1024], F32))
        p_sx = _ps.enter_context(nc.psum_tensor(_u("p_sx"), [1, 1024], F32))
        p_sq = _ps.enter_context(nc.psum_tensor(_u("p_sq"), [1, 1024], F32))
        p_b0 = _ps.enter_context(nc.psum_tensor(_u("p_b0"), [128, 512], F32))
        p_b1 = _ps.enter_context(nc.psum_tensor(_u("p_b1"), [128, 512], F32))
        p_mr = _ps.enter_context(nc.psum_tensor(_u("p_mr"), [128, 16], F32))
        s_sq = Sem(nc, "ln_sq")
        s_pe = Sem(nc, "ln_pe")
        s_ve = Sem(nc, "ln_ve")
        s_se = Sem(nc, "ln_se")

        with nc.Block(_u("ln_stats")) as blk:
            @blk.scalar
            def _(e):
                for k in range(NK):
                    if k >= 2:
                        s_pe.wait(e, 2 * (k - 1))
                    e.activation(
                        out=xb_sc[:, (k % 2) * 1024:(k % 2) * 1024 + 1024],
                        in_=xT[:, k * T:k * T + T], func=AF.Copy)
                    s_sq.inc(e.activation(
                        out=sq_sc[:, (k % 2) * 1024:(k % 2) * 1024 + 1024],
                        in_=xT[:, k * T:k * T + T], func=AF.Square))

            @blk.tensor
            def _(e):
                for k in range(NK):
                    st, sp = (k == 0), (k == NK - 1)
                    s_sq.wait(e, k + 1)
                    e.matmul(p_sx[0:1, 0:512], C.ones_col_b[:, :],
                             xb_sc[:, (k % 2) * 1024:(k % 2) * 1024 + 512],
                             start=st, stop=sp)
                    e.matmul(p_sx[0:1, 512:1024], C.ones_col_b[:, :],
                             xb_sc[:, (k % 2) * 1024 + 512:(k % 2) * 1024 + 1024],
                             start=st, stop=sp)
                    e.matmul(p_sq[0:1, 0:512], C.ones_col_b[:, :],
                             sq_sc[:, (k % 2) * 1024:(k % 2) * 1024 + 512],
                             start=st, stop=sp)
                    ins = e.matmul(
                        p_sq[0:1, 512:1024], C.ones_col_b[:, :],
                        sq_sc[:, (k % 2) * 1024 + 512:(k % 2) * 1024 + 1024],
                        start=st, stop=sp)
                    s_pe.inc(ins, 2)

        m_row = rows[0:1, 0:1024]
        r_row = rows[0:1, 1024:2048]
        ex2 = rows[0:1, 2048:3072]
        m2 = rows[0:1, 1024:2048]
        var = rows[0:1, 2048:3072]
        sd = rows[0:1, 1024:2048]
        with nc.Block(_u("ln_fin")) as blk:
            @blk.vector
            def _(e):
                s_ve.inc(e.tensor_scalar_mul(m_row, p_sx[0:1, :], 1.0 / D))
                s_ve.inc(e.tensor_scalar_mul(ex2, p_sq[0:1, :], 1.0 / D))
                s_ve.wait(e)
                s_ve.inc(e.tensor_mul(m2, m_row, m_row))
                s_ve.wait(e)
                s_ve.inc(e.tensor_sub(var, ex2, m2))

            @blk.scalar
            def _(e):
                s_ve.wait(e)
                s_se.inc(e.activation(out=sd, in_=var, func=AF.Sqrt,
                                      bias=C.eps_t[0:1, 0:1]))

            @blk.vector
            def _(e):
                s_se.wait(e)
                s_ve.inc(e.reciprocal(r_row, sd))

        with nc.Block(_u("ln_bc")) as blk:
            @blk.tensor
            def _(e):
                for tc in range(TC):
                    e.matmul(p_mr[:, tc:tc + 1],
                             m_row[0:1, 128 * tc:128 * tc + 128],
                             C.one1_f[0:1, 0:1], start=True, stop=True)
                    e.matmul(p_mr[:, TC + tc:TC + tc + 1],
                             r_row[0:1, 128 * tc:128 * tc + 128],
                             C.one1_f[0:1, 0:1], start=True, stop=True)
                e.matmul(p_b0[:, :], C.ones_row_f[0:1, :], m_row[0:1, 0:512],
                         start=True, stop=True)
                s_pe.inc(e.matmul(p_b1[:, :], C.ones_row_f[0:1, :],
                                  m_row[0:1, 512:1024], start=True, stop=True))

            @blk.vector
            def _(e):
                s_pe.wait(e)
                s_ve.inc(e.tensor_copy(out=mrcol[:, :], in_=p_mr[:, :]))
                s_ve.inc(e.tensor_copy(out=mB[:, 0:512], in_=p_b0[:, :]))
                s_ve.inc(e.tensor_copy(out=mB[:, 512:1024], in_=p_b1[:, :]))

            @blk.tensor
            def _(e):
                s_ve.wait(e)
                e.matmul(p_b0[:, :], C.ones_row_f[0:1, :], r_row[0:1, 0:512],
                         start=True, stop=True)
                s_pe.inc(e.matmul(p_b1[:, :], C.ones_row_f[0:1, :],
                                  r_row[0:1, 512:1024], start=True, stop=True))

            @blk.vector
            def _(e):
                s_pe.wait(e)
                s_ve.inc(e.tensor_copy(out=rB[:, 0:512], in_=p_b0[:, :]))
                s_ve.inc(e.tensor_copy(out=rB[:, 512:1024], in_=p_b1[:, :]))

        _ps.close()

        def norm_ops(e, ks, sem, half, buf=None):
            b = tmp_n if buf is None else buf
            for k in ks:
                sem.inc(e.tensor_sub(b[:, half * 1024:half * 1024 + 1024],
                                     xT[:, k * T:(k + 1) * T], mB[:, :]))
                sem.wait(e)
                sem.inc(e.tensor_mul(xlnT[:, k * T:(k + 1) * T],
                                     b[:, half * 1024:half * 1024 + 1024],
                                     rB[:, :]))
                sem.wait(e)

        if gate is None:
            with nc.Block(_u("ln_norm")) as blk:
                s_n = Sem(nc, "ln_n")
                s_n2 = Sem(nc, "ln_n2")

                @blk.vector
                def _(e):
                    norm_ops(e, range(0, 10), s_n, 0)

                @blk.gpsimd
                def _(e):
                    norm_ops(e, range(10, NK), s_n2, 1)
            return

        wg_sb, ind, indtv = gate["wg_sb"], gate["ind"], gate["indtv"]
        with (
            nc.psum_tensor(_u("p_ws"), [1, 16], F32) as p_ws,
            nc.psum_tensor(_u("p_wb"), [128, 16], F32) as p_wb,
            nc.psum_tensor(_u("p_gl"), [128, 8 * 16], F32) as p_gl,
            nc.sbuf_tensor(_u("g_sc"), [128, 512], F32) as g_sc,
            nc.sbuf_tensor(_u("gn_sc"), [128, 1024], F32) as gn_sc,
            nc.sbuf_tensor(_u("ws_row"), [1, 16], F32) as ws_row,
            nc.sbuf_tensor(_u("wsB"), [128, 16], F32) as wsB,
        ):
            raw = g_sc[:, 0:128]
            lgs = g_sc[:, 128:256]
            nlmax = g_sc[:, 256:264]
            exs = g_sc[:, 272:400]
            ssum = g_sc[:, 400:408]
            tvc = g_sc[:, 408:416]
            t2 = g_sc[:, 432:448]
            nmcol = g_sc[:, 448:456]
            s_g1 = Sem(nc, "g1")
            s_g2 = Sem(nc, "g2")
            s_ga = Sem(nc, "ga")
            s_gb = Sem(nc, "gb")
            s_gc = Sem(nc, "gc")
            s_n2 = Sem(nc, "ln_n2")
            with nc.Block(_u("norm_gate")) as blk:
                @blk.gpsimd
                def _(e):
                    norm_ops(e, range(0, 12), s_n2, 0)

                @blk.tensor
                def _(e):
                    ins = None
                    for k in range(NK):
                        ins = e.matmul(p_ws[0:1, :], C.ones_col_f[:, :],
                                       wg_sb[:, 16 * k:16 * k + 16],
                                       start=(k == 0), stop=(k == NK - 1))
                    s_g1.inc(ins)
                    ins = None
                    for tc in range(TC):
                        for k in range(NK):
                            ins = e.matmul(
                                p_gl[:, 16 * tc:16 * tc + 16],
                                xT[:, k * T + 128 * tc:k * T + 128 * tc + 128],
                                wg_sb[:, 16 * k:16 * k + 16],
                                start=(k == 0), stop=(k == NK - 1))
                    s_ga.inc(ins)

                @blk.vector
                def _(e):
                    s_g1.wait(e, 1)
                    s_g2.inc(e.tensor_copy(out=ws_row[:, :], in_=p_ws[0:1, :]))

                @blk.tensor
                def _(e):
                    s_g2.wait(e, 1)
                    s_g1.inc(e.matmul(p_wb[:, :], C.ones_row_f[0:1, :],
                                      ws_row[0:1, :], start=True, stop=True))

                @blk.vector
                def _(e):
                    s_g1.wait(e, 2)
                    s_g2.inc(e.tensor_copy(out=wsB[:, :], in_=p_wb[:, :]))
                    s_g2.inc(e.tensor_scalar_mul(nmcol, mrcol[:, 0:TC], -1.0))
                    s_g2.wait(e)
                    s_ga.wait(e)
                    for tc in range(TC):
                        sl = slice(16 * tc, 16 * tc + 16)
                        s_gb.inc(e.scalar_tensor_tensor(
                            t2, wsB[:, :], nmcol[:, tc:tc + 1], p_gl[:, sl],
                            ALU.mult, ALU.add))
                        s_gb.wait(e)
                        s_gb.inc(e.tensor_scalar(lgs[:, sl], t2,
                                                 mrcol[:, TC + tc:TC + tc + 1],
                                                 None, ALU.mult))
                        s_gb.wait(e)
                        s_gb.inc(e.reduce_max(nlmax[:, tc:tc + 1], lgs[:, sl],
                                              axis=AX.X, negate=True))
                        s_gb.wait(e)
                        s_gb.inc(e.tensor_single_scalar(
                            ind[:, sl], lgs[:, sl], nlmax[:, tc:tc + 1],
                            ALU.add))
                        s_gb.wait(e)
                        s_gb.inc(e.tensor_single_scalar(
                            ind[:, sl], ind[:, sl], 0.0, ALU.is_ge))
                        s_gb.wait(e)

                @blk.scalar
                def _(e):
                    s_gb.wait(e)
                    ins = None
                    for tc in range(TC):
                        sl = slice(16 * tc, 16 * tc + 16)
                        ins = e.activation(out=exs[:, sl], in_=lgs[:, sl],
                                           func=AF.Exp,
                                           bias=nlmax[:, tc:tc + 1])
                    s_gc.inc(ins)

                @blk.vector
                def _(e):
                    s_gc.wait(e)
                    for tc in range(TC):
                        sl = slice(16 * tc, 16 * tc + 16)
                        s_gb.inc(e.reduce_sum(ssum[:, tc:tc + 1], exs[:, sl],
                                              axis=AX.X))
                        s_gb.wait(e)
                        s_gb.inc(e.reciprocal(tvc[:, tc:tc + 1],
                                              ssum[:, tc:tc + 1]))
                        s_gb.wait(e)
                        s_gb.inc(e.tensor_scalar(indtv[:, sl], ind[:, sl],
                                                 tvc[:, tc:tc + 1], None,
                                                 ALU.mult))
                        s_gb.wait(e)
                    s_ngd = Sem(nc, "ngd")
                    norm_ops(e, range(12, NK), s_ngd, 0, buf=gn_sc)



def emit_proj_feat(nc, C, w_dram, n_oc, srcT, out_dram, out_f32_add=None,
                   out_dram_f32=None):
    wbf = w_dram.dtype == BF16
    with (
        nc.psum_tensor(_u("pp0"), [128, 1024], F32) as pp0,
        nc.psum_tensor(_u("pp1"), [128, 1024], F32) as pp1,
        nc.sbuf_tensor(_u("wf"), [128, 2 * NK * 128],
                       BF16 if wbf else F32) as wf,
        nc.sbuf_tensor(_u("wb"), [128, 3 * NK * 128], BF16) as wb,
        nc.sbuf_tensor(_u("stg"), [128, 2 * 1024],
                       BF16 if out_f32_add is None else F32) as stg,
    ):
        if wbf:
            wb = wf
        s_wd = DSem(nc, "pf_wd")
        s_wc = Sem(nc, "pf_wc")
        s_pe = Sem(nc, "pf_pe")
        s_ev = Sem(nc, "pf_ev")
        s_st = DSem(nc, "pf_st")
        s_wb = Sem(nc, "pf_wb")
        pp = [pp0, pp1]
        with nc.Block(_u("proj_feat")) as blk:
            @blk.sync
            def _(e):
                for oc in range(n_oc):
                    par = oc % 2
                    if oc >= 2:
                        if wbf:
                            s_pe.wait(e, oc - 1)
                        else:
                            s_wc.wait(e, oc - 1)
                    s_wd.inc(oc, e.dma_start(
                        out=wf[:, par * NK * 128:(par + 1) * NK * 128]
                        .rearrange("p (k o) -> p k o", o=128),
                        in_=w_dram[:, 128 * oc:128 * oc + 128]
                        .rearrange("(k p) o -> p k o", p=128)))

            @blk.gpsimd
            def _(e):
                if wbf:
                    return
                for oc in range(n_oc):
                    par = oc % 2
                    par3 = oc % 3
                    s_wd.wait_done(e, oc)
                    if oc >= 3:
                        s_pe.wait(e, oc - 2)
                    s_wc.inc(e.tensor_copy(
                        out=wb[:, par3 * NK * 128:(par3 + 1) * NK * 128],
                        in_=wf[:, par * NK * 128:(par + 1) * NK * 128]))

            @blk.tensor
            def _(e):
                for oc in range(n_oc):
                    par = oc % 2
                    if wbf:
                        s_wd.wait_done(e, oc)
                        wbase = par * NK * 128
                    else:
                        s_wc.wait(e, oc + 1)
                        wbase = (oc % 3) * NK * 128
                    if oc >= 2:
                        s_ev.wait(e, oc - 1)
                    ins = None
                    for k in range(NK):
                        st, sp = (k == 0), (k == NK - 1)
                        base = wbase + 128 * k
                        e.matmul(pp[par][:, 0:512], wb[:, base:base + 128],
                                 srcT[:, k * T:k * T + 512], start=st, stop=sp)
                        ins = e.matmul(pp[par][:, 512:1024],
                                       wb[:, base:base + 128],
                                       srcT[:, k * T + 512:k * T + 1024],
                                       start=st, stop=sp)
                    s_pe.inc(ins)

            @blk.vector
            def _(e):
                for oc in range(n_oc):
                    par = oc % 2
                    s_pe.wait(e, oc + 1)
                    if oc >= 2:
                        s_st.wait_done(e, oc - 2)
                        if out_f32_add is not None:
                            s_wb.wait(e, oc - 1)
                    if out_f32_add is None:
                        s_ev.inc(e.tensor_copy(
                            out=stg[:, par * 1024:(par + 1) * 1024],
                            in_=pp[par][:, :]))
                    else:
                        s_ev.inc(e.tensor_add(
                            stg[:, par * 1024:(par + 1) * 1024],
                            pp[par][:, :],
                            out_f32_add[:, oc * T:(oc + 1) * T]))

            @blk.scalar
            def _(e):
                dram = out_dram if out_dram_f32 is None else out_dram_f32
                for oc in range(n_oc):
                    par = oc % 2
                    s_ev.wait(e, oc + 1)
                    if out_f32_add is not None:
                        s_wb.inc(e.copy(
                            out=out_f32_add[:, oc * T:(oc + 1) * T],
                            in_=stg[:, par * 1024:(par + 1) * 1024]))
                    s_st.inc(oc, e.dma_start(
                        out=dram[128 * oc:128 * oc + 128, :],
                        in_=stg[:, par * 1024:(par + 1) * 1024]))
                s_st.wait_all(e)


def emit_proj_tok(nc, C, w_dram, n_cg, srcT, consume):
    wbf = w_dram.dtype == BF16
    with (
        nc.psum_tensor(_u("pt0"), [128, 512], F32) as b0,
        nc.psum_tensor(_u("pt1"), [128, 512], F32) as b1,
        nc.psum_tensor(_u("pt2"), [128, 512], F32) as b2,
        nc.psum_tensor(_u("pt3"), [128, 512], F32) as b3,
        nc.psum_tensor(_u("pt4"), [128, 512], F32) as b4,
        nc.psum_tensor(_u("pt5"), [128, 512], F32) as b5,
        nc.psum_tensor(_u("pt6"), [128, 512], F32) as b6,
        nc.psum_tensor(_u("pt7"), [128, 512], F32) as b7,
        nc.sbuf_tensor(_u("wtf"), [128, 2 * 8 * 512],
                       BF16 if wbf else F32) as wtf,
        nc.sbuf_tensor(_u("wtb"), [128, 4 * 512], BF16) as wtb,
    ):
        banks = [b0, b1, b2, b3, b4, b5, b6, b7]
        s_wd = DSem(nc, "pt_wd")
        s_wc = Sem(nc, "pt_wc")
        s_pe = Sem(nc, "pt_pe")
        s_ev = Sem(nc, "pt_ev")
        KG = NK // 8
        with nc.Block(_u("proj_tok")) as blk:
            @blk.sync
            def _(e):
                for cg in range(n_cg):
                    for kg in range(KG):
                        i2 = cg * KG + kg
                        par = i2 % 2
                        if i2 >= 2:
                            if wbf:
                                s_pe.wait(e, 8 * (i2 - 1))
                            else:
                                s_wc.wait(e, 8 * (i2 - 1))
                        s_wd.inc(i2, e.dma_start(
                            out=wtf[:, par * 8 * 512:(par + 1) * 8 * 512]
                            .rearrange("p (k c) -> p k c", c=512),
                            in_=w_dram[1024 * kg:1024 * kg + 1024,
                                       512 * cg:512 * cg + 512]
                            .rearrange("(k p) c -> p k c", p=128)))

            @blk.scalar
            def _(e):
                if wbf:
                    return
                for cg in range(n_cg):
                    for k in range(NK):
                        i = cg * NK + k
                        par = i % 4
                        i2 = cg * KG + k // 8
                        s_wd.wait_done(e, i2)
                        if i >= 4:
                            s_pe.wait(e, i - 3)
                        s_wc.inc(e.copy(
                            out=wtb[:, par * 512:(par + 1) * 512],
                            in_=wtf[:, (i2 % 2) * 8 * 512 + (k % 8) * 512:
                                    (i2 % 2) * 8 * 512 + (k % 8) * 512 + 512]))

            @blk.tensor
            def _(e):
                for cg in range(n_cg):
                    for k in range(NK):
                        i = cg * NK + k
                        par = i % 2
                        i2 = cg * KG + k // 8
                        if wbf:
                            s_wd.wait_done(e, i2)
                            wop = wtf[:, (i2 % 2) * 8 * 512 + (k % 8) * 512:
                                      (i2 % 2) * 8 * 512 + (k % 8) * 512 + 512]
                        else:
                            s_wc.wait(e, i + 1)
                            wop = wtb[:, (i % 4) * 512:(i % 4) * 512 + 512]
                        if k == 0 and cg >= 1:
                            s_ev.wait(e, 8 * cg)
                        ins = None
                        for tc in range(TC):
                            ins = e.matmul(
                                banks[tc][:, :],
                                srcT[:, k * T + 128 * tc:k * T + 128 * tc + 128],
                                wop,
                                start=(k == 0), stop=(k == NK - 1))
                        s_pe.inc(ins)

            @blk.vector
            def _(e):
                for cg in range(n_cg):
                    s_pe.wait(e, NK * (cg + 1))
                    consume(e, cg, banks, s_ev)




def proj_tok_res_alloc(nc):
    return nc.alloc_sbuf_tensor(_u("wres"), [128, 2 * NK * 1024], BF16)


def proj_tok_res_prefetch(nc, w_dram, wres, s_wd):
    n_pass = w_dram.shape[1] // 1024
    with nc.Block(_u("wpref")) as blk:
        @blk.sync
        def _(e):
            for p in range(min(2, n_pass)):
                s_wd.inc(p, e.dma_start(
                    out=wres[:, (p % 2) * NK * 1024:(p % 2 + 1) * NK * 1024]
                    .rearrange("p (k c) -> p k c", c=1024),
                    in_=w_dram[:, 1024 * p:1024 * p + 1024]
                    .rearrange("(k p) c -> p k c", p=128)))


def emit_proj_tok_res(nc, C, w_dram, srcT, consume_tc, wres, s_wd,
                      dma_out=None):
    n_pass = w_dram.shape[1] // 1024
    with (
        nc.psum_tensor(_u("pr0"), [128, 512], F32) as b0,
        nc.psum_tensor(_u("pr1"), [128, 512], F32) as b1,
        nc.psum_tensor(_u("pr2"), [128, 512], F32) as b2,
        nc.psum_tensor(_u("pr3"), [128, 512], F32) as b3,
        nc.psum_tensor(_u("pr4"), [128, 512], F32) as b4,
        nc.psum_tensor(_u("pr5"), [128, 512], F32) as b5,
        nc.psum_tensor(_u("pr6"), [128, 512], F32) as b6,
        nc.psum_tensor(_u("pr7"), [128, 512], F32) as b7,
    ):
        banks = [b0, b1, b2, b3, b4, b5, b6, b7]
        s_pe = Sem(nc, "pr_pe")
        s_ev = Sem(nc, "pr_ev")
        with nc.Block(_u("proj_res")) as blk:
            @blk.sync
            def _(e):
                for p in range(2, n_pass):
                    s_pe.wait(e, 8 * (p - 1))
                    s_wd.inc(p, e.dma_start(
                        out=wres[:, (p % 2) * NK * 1024:(p % 2 + 1) * NK * 1024]
                        .rearrange("p (k c) -> p k c", c=1024),
                        in_=w_dram[:, 1024 * p:1024 * p + 1024]
                        .rearrange("(k p) c -> p k c", p=128)))

            @blk.tensor
            def _(e):
                for p in range(n_pass):
                    par = p % 2
                    s_wd.wait_done(e, p)
                    for tc in range(TC):
                        idx = p * TC + tc
                        if idx >= 4:
                            s_ev.wait(e, 2 * (idx - 3))
                        bb = [banks[(tc % 4) * 2], banks[(tc % 4) * 2 + 1]]
                        ins = None
                        for k in range(NK):
                            wof = par * NK * 1024 + k * 1024
                            for cg2 in range(2):
                                ins = e.matmul(
                                    bb[cg2][:, :],
                                    srcT[:, k * T + 128 * tc:
                                         k * T + 128 * tc + 128],
                                    wres[:, wof + 512 * cg2:wof + 512 * cg2 + 512],
                                    start=(k == 0), stop=(k == NK - 1))
                        s_pe.inc(ins)

            @blk.vector
            def _(e):
                for p in range(n_pass):
                    for tc in range(TC):
                        idx = p * TC + tc
                        s_pe.wait(e, idx + 1)
                        bb = [banks[(tc % 4) * 2], banks[(tc % 4) * 2 + 1]]
                        consume_tc(e, p, tc, bb, s_ev)

            if dma_out is not None:
                stage, out_dram, s_st = dma_out

                @blk.scalar
                def _(e):
                    for p in range(n_pass):
                        for tc in range(TC):
                            for cg2 in range(2):
                                i = (p * TC + tc) * 2 + cg2
                                s_ev.wait(e, i + 1)
                                s_st.inc(i, e.dma_start(
                                    out=out_dram[
                                        128 * tc:128 * tc + 128,
                                        1024 * p + 512 * cg2:
                                        1024 * p + 512 * cg2 + 512],
                                    in_=stage[:, 512 * (i % 2):
                                              512 * (i % 2) + 512]))
                    s_st.wait_all(e)


def selection_consume_tc(kacc, vacc, ind, indtv, s_chain):
    def consume(e, p, tc, bb, s_ev):
        for cg2 in range(2):
            colbase = 1024 * p + 512 * cg2
            kpart = colbase < D
            acc = kacc if kpart else vacc
            wsel = ind if kpart else indtv
            for hl in range(4):
                h = ((colbase % D) // 128) + hl
                first = (colbase % D == 0 and hl == 0)
                a = acc[:, 128 * tc:128 * tc + 128]
                pp = bb[cg2][:, 128 * hl:128 * hl + 128]
                wcol = wsel[:, 16 * tc + h:16 * tc + h + 1]
                if first:
                    ins = e.tensor_scalar(a, pp, wcol, None, ALU.mult)
                else:
                    ins = e.scalar_tensor_tensor(a, pp, wcol, a,
                                                 ALU.mult, ALU.add)
                s_chain.inc(ins)
                s_chain.wait(e)
            s_ev.inc(e.engine_nop())
    return consume


def evac_stage_consume_tc(stage, s_st):
    ctr = [0]

    def consume(e, p, tc, bb, s_ev):
        for cg2 in range(2):
            i = ctr[0]
            ctr[0] += 1
            slot = i % 2
            s_st.wait_done(e, i - 2)
            s_ev.inc(e.tensor_copy(
                out=stage[:, 512 * slot:512 * slot + 512], in_=bb[cg2][:, :]))
    return consume


def evac_consume_tc(out_sb, width):
    def consume(e, p, tc, bb, s_ev):
        for cg2 in range(2):
            col = 1024 * p + 512 * cg2
            s_ev.inc(e.tensor_copy(
                out=out_sb[:, tc * width + col:tc * width + col + 512],
                in_=bb[cg2][:, :]))
    return consume


def selection_consume(kacc, vacc, ind, indtv, s_chain):
    def consume(e, cg, banks, s_ev):
        kpart = cg < 4
        acc = kacc if kpart else vacc
        wsel = ind if kpart else indtv
        for hl in range(4):
            for tc in range(TC):
                h = 4 * (cg % 4) + hl
                first = (cg % 4 == 0 and hl == 0)
                a = acc[:, 128 * tc:128 * tc + 128]
                p = banks[tc][:, 128 * hl:128 * hl + 128]
                wcol = wsel[:, 16 * tc + h:16 * tc + h + 1]
                if not first and s_chain.v >= TC:
                    s_chain.wait(e, s_chain.v - (TC - 1))
                if first:
                    ins = e.tensor_scalar(a, p, wcol, None, ALU.mult)
                else:
                    ins = e.scalar_tensor_tensor(a, p, wcol, a,
                                                 ALU.mult, ALU.add)
                s_chain.inc(ins)
                if hl == 3:
                    s_ev.inc(e.engine_nop())
    return consume


def evac_consume(out_sb, width):
    def consume(e, cg, banks, s_ev):
        for tc in range(TC):
            s_ev.inc(e.tensor_copy(
                out=out_sb[:, tc * width + 512 * cg:tc * width + 512 * cg + 512],
                in_=banks[tc][:, :]))
    return consume



def emit_flash2(nc, C, qhT, khT, kh_stride, vt, vt_stride, vt_hoff, masks,
                scale_post, fy_dtype, y_out_fn, add_src=None,
                side_section=None):
    with (
        nc.psum_tensor(_u("fS0"), [128, 1024], F32) as S0,
        nc.psum_tensor(_u("fS1"), [128, 1024], F32) as S1,
        nc.psum_tensor(_u("fY0"), [128, 512], F32) as Y0,
        nc.psum_tensor(_u("fY1"), [128, 512], F32) as Y1,
        nc.psum_tensor(_u("fL0"), [128, 512], F32) as L0,
        nc.psum_tensor(_u("fL1"), [128, 512], F32) as L1,
        nc.sbuf_tensor(_u("fpT"), [128, 4 * 1024], BF16) as pT,
        nc.sbuf_tensor(_u("frl"), [1, 2 * 512], F32) as rl,
        nc.sbuf_tensor(_u("frb"), [128, 2 * 512], F32) as rb,
        nc.sbuf_tensor(_u("ffy"), [128, 2 * 512], fy_dtype) as fy,
    ):
        Sp = [S0, S1]
        Yp = [Y0, Y1]
        Lp = [L0, L1]
        s_S = Sem(nc, "f_S")
        s_E = Sem(nc, "f_E")
        s_M = Sem(nc, "f_M")
        s_PV = Sem(nc, "f_PV")
        s_R = Sem(nc, "f_R")
        s_B = Sem(nc, "f_B")
        s_YE = Sem(nc, "f_YE")
        s_Y = DSem(nc, "f_Y")

        groups = []
        qblocks = []
        gidx = 0
        qidx = 0
        for hl in range(HPC):
            for qb in range(NQB):
                ng = 2 * qb + 2
                qblocks.append((hl, qb, ng, gidx, qidx))
                for g in range(ng):
                    groups.append((hl, qb, g, g >= ng - 2, gidx))
                    gidx += 1
                qidx += 1
        nmask_upto = {}
        acc = 0
        for (hl, qb, g, masked, gi) in groups:
            if masked:
                acc += 1
            nmask_upto[gi] = acc
        g2q = {}
        for (hl, qb, ng, g0, qi) in qblocks:
            for g in range(ng):
                g2q[g0 + g] = qi

        with nc.Block(_u("flash")) as blk:
            if side_section is not None:
                side_section(blk)

            @blk.tensor
            def _(e):
                def emit_S(gi):
                    (hl, qb, g, masked, _) = groups[gi]
                    sp = gi % 2
                    qs = hl * S + 512 * qb
                    ins = None
                    for half in range(2):
                        kc = 2 * g + half
                        kb = hl * kh_stride + 128 * kc
                        ins = e.matmul(Sp[sp][:, 512 * half:512 * half + 512],
                                       khT[:, kb:kb + 128],
                                       qhT[:, qs:qs + 512],
                                       start=True, stop=True)
                    s_S.inc(ins)

                def emit_Y(gi):
                    (hl, qb, g, masked, _) = groups[gi]
                    sp = gi % 4
                    qi = g2q[gi]
                    yb = qi % 2
                    ng = 2 * qb + 2
                    pos = g
                    if pos == 0 and qi >= 2:
                        s_YE.wait(e, 2 * (qi - 1))
                    for half in range(2):
                        kc = 2 * g + half
                        vb = kc * vt_stride + vt_hoff * hl
                        e.matmul(Yp[yb][:, :], vt[:, vb:vb + 128],
                                 pT[:, sp * 1024 + 512 * half:
                                    sp * 1024 + 512 * half + 512],
                                 start=(pos == 0 and half == 0),
                                 stop=(pos == ng - 1 and half == 1))

                def emit_L(gi):
                    (hl, qb, g, masked, _) = groups[gi]
                    sp = gi % 4
                    qi = g2q[gi]
                    yb = qi % 2
                    ng = 2 * qb + 2
                    pos = g
                    ins = None
                    for half in range(2):
                        ins = e.matmul(Lp[yb][0:1, :], C.ones_col_b[:, :],
                                       pT[:, sp * 1024 + 512 * half:
                                          sp * 1024 + 512 * half + 512],
                                       start=(pos == 0 and half == 0),
                                       stop=(pos == ng - 1 and half == 1))
                    if pos == ng - 1:
                        s_PV.inc(ins)

                def emit_BC(qi):
                    yb = qi % 2
                    s_R.wait(e, qi + 1)
                    s_B.inc(e.matmul(Lp[yb][:, :], C.ones_row_f[0:1, :],
                                     rl[0:1, 512 * yb:512 * yb + 512],
                                     start=True, stop=True))

                NG = len(groups)
                NB = NG // 2
                pending_bc = []
                def emit_S_batch(b):
                    if b >= NB:
                        return
                    for gi in (2 * b, 2 * b + 1):
                        if gi >= 2:
                            s_E.wait(e, gi - 1)
                        emit_S(gi)

                def emit_PV_batch(b):
                    if b == NB - 1:
                        s_E.wait(e, 2 * NB)
                    mneed = 0
                    for gi in (2 * b, 2 * b + 1):
                        if groups[gi][3]:
                            mneed = max(mneed, nmask_upto[gi])
                    if mneed:
                        s_M.wait(e, mneed)
                    for gi in (2 * b, 2 * b + 1):
                        emit_Y(gi)
                    for gi in (2 * b, 2 * b + 1):
                        emit_L(gi)

                emit_S_batch(0)
                for b in range(NB):
                    emit_S_batch(b + 1)
                    while pending_bc:
                        emit_BC(pending_bc.pop(0))
                    emit_PV_batch(b)
                    if groups[2 * b + 1][2] == 2 * groups[2 * b + 1][1] + 1:
                        pending_bc.append(g2q[2 * b + 1])
                while pending_bc:
                    emit_BC(pending_bc.pop(0))

            @blk.scalar
            def _(e):
                for (hl, qb, g, masked, gi) in groups:
                    sp4 = gi % 4
                    s_S.wait(e, gi + 1)
                    s_E.inc(e.activation(out=pT[:, sp4 * 1024:sp4 * 1024 + 1024],
                                         in_=Sp[gi % 2][:, :], func=AF.Exp,
                                         bias=C.bias_sh[:, 0:1],
                                         scale=SM_SCALE))

            @blk.vector
            def _(e):
                s_rl = Sem(nc, "f_rl")
                for (hl, qb, ng, g0, qi) in qblocks:
                    yb = qi % 2
                    for gi in (g0 + ng - 2, g0 + ng - 1):
                        g = groups[gi][2]
                        sp = gi % 4
                        s_E.wait(e, gi + 1)
                        mask = masks[0] if g == ng - 2 else masks[1]
                        s_M.inc(e.tensor_mul(
                            pT[:, sp * 1024:sp * 1024 + 1024],
                            pT[:, sp * 1024:sp * 1024 + 1024], mask[:, :]))
                    s_PV.wait(e, qi + 1)
                    s_rl.inc(e.reciprocal(rl[0:1, 512 * yb:512 * yb + 512],
                                          Lp[yb][0:1, :]))
                    s_rl.wait(e)
                    if scale_post != 1.0:
                        s_rl.inc(e.tensor_scalar_mul(
                            rl[0:1, 512 * yb:512 * yb + 512],
                            rl[0:1, 512 * yb:512 * yb + 512],
                            float(scale_post)))
                        s_rl.wait(e)
                    s_R.inc(e.engine_nop())
                    s_B.wait(e, qi + 1)
                    s_Y.wait_done(e, qi - 2)
                    s_YE.inc(e.tensor_copy(out=rb[:, 512 * yb:512 * yb + 512],
                                           in_=Lp[yb][:, :]))
                    s_YE.wait(e)
                    if add_src is None:
                        s_YE.inc(e.tensor_mul(fy[:, 512 * yb:512 * yb + 512],
                                              Yp[yb][:, :],
                                              rb[:, 512 * yb:512 * yb + 512]))
                    else:
                        s_rl.inc(e.tensor_mul(rb[:, 512 * yb:512 * yb + 512],
                                              Yp[yb][:, :],
                                              rb[:, 512 * yb:512 * yb + 512]))
                        s_rl.wait(e)
                        s_YE.inc(e.tensor_add(
                            fy[:, 512 * yb:512 * yb + 512],
                            add_src[:, hl * S + 512 * qb:hl * S + 512 * qb + 512],
                            rb[:, 512 * yb:512 * yb + 512]))

            @blk.sync
            def _(e):
                for (hl, qb, ng, g0, qi) in qblocks:
                    yb = qi % 2
                    s_YE.wait(e, 2 * (qi + 1))
                    y_out_fn(e, hl, qb, qi, fy[:, 512 * yb:512 * yb + 512], s_Y)
                s_Y.wait_all(e)


def emit_transpose_seq(nc, C, src, dst, n, ident):
    with (
        nc.psum_tensor(_u("ptr0"), [128, 128], BF16) as q0,
        nc.psum_tensor(_u("ptr1"), [128, 128], BF16) as q1,
    ):
        qp = [q0, q1]
        s_t = Sem(nc, "tr_t")
        s_c = Sem(nc, "tr_c")
        with nc.Block(_u("transpose")) as blk:
            @blk.tensor
            def _(e):
                for j in range(n):
                    par = j % 2
                    if j >= 2:
                        s_c.wait(e, j - 1)
                    s_t.inc(e.transpose(qp[par][:, :],
                                        src[:, 128 * j:128 * j + 128],
                                        ident[:, :]))

            @blk.vector
            def _(e):
                for j in range(n):
                    par = j % 2
                    s_t.wait(e, j + 1)
                    s_c.inc(e.tensor_copy(out=dst[:, 128 * j:128 * j + 128],
                                          in_=qp[par][:, :]))




def make_side_cast(nc, pairs, chunk_cols=2048):
    scf = nc.alloc_sbuf_tensor(_u("scf"), [128, 2 * chunk_cols], F32)
    scb = nc.alloc_sbuf_tensor(_u("scb"), [128, 2 * chunk_cols], BF16)
    s_d = DSem(nc, "sc_d")
    s_c = Sem(nc, "sc_c")
    s_o = DSem(nc, "sc_o")
    jobs = []
    for (srcp, dstp) in pairs:
        R, Cc = srcp.shape
        assert Cc == chunk_cols and R % 128 == 0
        for r in range(R // 128):
            jobs.append((srcp, dstp, r))

    def section(blk):
        @blk.gpsimd
        def _(e):
            def load(i):
                if i >= len(jobs):
                    return
                srcp, dstp, r = jobs[i]
                s_d.inc(i, e.dma_start(
                    out=scf[:, (i % 2) * chunk_cols:(i % 2 + 1) * chunk_cols],
                    in_=srcp[128 * r:128 * r + 128, :]))
            load(0)
            load(1)
            for i, (srcp, dstp, r) in enumerate(jobs):
                par = i % 2
                s_d.wait_done(e, i)
                s_o.wait_done(e, i - 2)
                s_c.inc(e.tensor_copy(
                    out=scb[:, par * chunk_cols:(par + 1) * chunk_cols],
                    in_=scf[:, par * chunk_cols:(par + 1) * chunk_cols]))
                s_c.wait(e)
                s_o.inc(i, e.dma_start(
                    out=dstp[128 * r:128 * r + 128, :],
                    in_=scb[:, par * chunk_cols:(par + 1) * chunk_cols]))
                load(i + 2)
            s_o.wait_all(e)
    return section



def build_prep1(with_carry=False):
    nc = bass.Bass("TRN2", target_bir_lowering=False)
    C = Consts(nc)
    x_t = nc.dram_tensor("x_t", [D, T], F32, kind="ExternalInput").ap()
    wg = nc.dram_tensor("wg", [D, 16], F32, kind="ExternalInput").ap()
    WDT = BF16 if with_carry else F32
    wkv = nc.dram_tensor("wkv", [D, 2 * D], WDT, kind="ExternalInput").ap()
    wq = nc.dram_tensor("wq", [D, D], WDT, kind="ExternalInput").ap()
    if with_carry:
        y2_t = nc.dram_tensor("y2_t", [D, T], BF16, kind="ExternalInput").ap()
        wo = nc.dram_tensor("wo", [D, D], BF16, kind="ExternalInput").ap()
        x2_t = nc.dram_tensor("x2_t", [D, T], F32, kind="ExternalOutput").ap()
    qt = nc.dram_tensor("qt", [D, T], BF16, kind="ExternalOutput").ap()
    ksht = nc.dram_tensor("ksht", [128, T], BF16, kind="ExternalOutput").ap()
    vsht = nc.dram_tensor("vsht", [T, 128], BF16, kind="ExternalOutput").ap()
    idn = nc.dram_tensor("idn", [128, 128], F32, kind="ExternalInput").ap()

    xT = nc.alloc_sbuf_tensor("xT", [128, NK * T], F32)
    xlnT = nc.alloc_sbuf_tensor("xlnT", [128, NK * T], BF16)
    mrcol = nc.alloc_sbuf_tensor("mrcol", [128, 16], F32)
    wg_sb = nc.alloc_sbuf_tensor("wg_sb", [128, NK * 16], F32)
    ind = nc.alloc_sbuf_tensor("ind_sb", [128, TC * 16], F32)
    indtv = nc.alloc_sbuf_tensor("indtv_sb", [128, TC * 16], F32)
    kacc = nc.alloc_sbuf_tensor("kacc", [128, TC * 128], F32)
    vacc = nc.alloc_sbuf_tensor("vacc", [128, TC * 128], F32)
    kvb = nc.alloc_sbuf_tensor("kvb", [128, TC * 128], BF16)
    ksb = nc.alloc_sbuf_tensor("ksb", [128, T], BF16)
    identb = nc.alloc_sbuf_tensor("identb", [128, 128], F32)

    C.emit(nc)
    loads = [
        (xT[:, :].rearrange("p (k t) -> p k t", t=T),
         x_t[:, :].rearrange("(k p) t -> p k t", p=128)),
        (wg_sb[:, :].rearrange("p (k h) -> p k h", h=16),
         wg[:, :].rearrange("(k p) h -> p k h", p=128)),
        (identb[:, :], idn[:, :]),
    ]
    if with_carry:
        loads.append((xlnT[:, :].rearrange("p (k t) -> p k t", t=T),
                      y2_t[:, :].rearrange("(k p) t -> p k t", p=128)))
        wres = proj_tok_res_alloc(nc)
        s_wkv = DSem(nc, "wkv")
    emit_loads(nc, loads)

    if with_carry:
        proj_tok_res_prefetch(nc, wkv, wres, s_wkv)
        emit_proj_feat(nc, C, wo, NK, xlnT, None, out_f32_add=xT,
                       out_dram_f32=x2_t)

    emit_ln(nc, C, xT, xlnT, mrcol,
            gate=dict(wg_sb=wg_sb, ind=ind, indtv=indtv))

    s_chain = Sem(nc, "selchain")
    if with_carry:
        emit_proj_tok_res(nc, C, wkv, xlnT,
                          selection_consume_tc(kacc, vacc, ind, indtv, s_chain),
                          wres, s_wkv)
    else:
        emit_proj_tok(nc, C, wkv, 8, xlnT,
                      selection_consume(kacc, vacc, ind, indtv, s_chain))

    s_kv = Sem(nc, "kv_cast")
    s_kvs = Sem(nc, "kv_st")
    s_ktr = Sem(nc, "kv_tr")
    with (
        nc.psum_tensor(_u("ptk0"), [128, 128], F32) as ptk0,
        nc.psum_tensor(_u("ptk1"), [128, 128], F32) as ptk1,
    ):
        ptk = [ptk0, ptk1]
        with nc.Block(_u("kv_out")) as blk:
            @blk.vector
            def _(e):
                for tc in range(TC):
                    s_kv.inc(e.tensor_copy(
                        out=kvb[:, 128 * tc:128 * tc + 128],
                        in_=vacc[:, 128 * tc:128 * tc + 128]))
                for tc in range(TC):
                    s_ktr.wait(e, tc + 1)
                    s_kv.inc(e.tensor_copy(
                        out=ksb[:, 128 * tc:128 * tc + 128],
                        in_=ptk[tc % 2][:, :]))

            @blk.tensor
            def _(e):
                for tc in range(TC):
                    if tc >= 2:
                        s_kv.wait(e, 8 + tc - 1)
                    s_ktr.inc(e.transpose(ptk[tc % 2][:, :],
                                          kacc[:, 128 * tc:128 * tc + 128],
                                          identb[:, :]))

            @blk.sync
            def _(e):
                s_kv.wait(e)
                s_kvs.inc(e.dma_start(
                    out=vsht[:, :].rearrange("(tc p) c -> p tc c", p=128),
                    in_=kvb[:, :].rearrange("p (tc c) -> p tc c", c=128)), 16)
                s_kvs.inc(e.dma_start(out=ksht[:, :], in_=ksb[:, :]), 16)
                s_kvs.wait(e)

    emit_proj_feat(nc, C, wq, NK, xlnT, qt)
    nc.finalize()
    return nc


def build_prep2():
    nc = bass.Bass("TRN2", target_bir_lowering=False)
    C = Consts(nc)
    x_t = nc.dram_tensor("x_t", [D, T], F32, kind="ExternalInput").ap()
    wq = nc.dram_tensor("wq", [D, D], BF16, kind="ExternalInput").ap()
    wk = nc.dram_tensor("wk", [D, D], BF16, kind="ExternalInput").ap()
    wv = nc.dram_tensor("wv", [D, D], BF16, kind="ExternalInput").ap()
    q2t = nc.dram_tensor("q2t", [D, T], BF16, kind="ExternalOutput").ap()
    k2t = nc.dram_tensor("k2t", [D, T], BF16, kind="ExternalOutput").ap()
    v2t = nc.dram_tensor("v2t", [T, D], BF16, kind="ExternalOutput").ap()

    xT = nc.alloc_sbuf_tensor("xT", [128, NK * T], F32)
    xlnT = nc.alloc_sbuf_tensor("xlnT", [128, NK * T], BF16)
    mrcol = nc.alloc_sbuf_tensor("mrcol", [128, 16], F32)
    vstg = nc.alloc_sbuf_tensor("vstg", [128, 2 * 512], BF16)

    C.emit(nc)
    wres = proj_tok_res_alloc(nc)
    s_wv = DSem(nc, "wv")
    emit_loads(nc, [
        (xT[:, :].rearrange("p (k t) -> p k t", t=T),
         x_t[:, :].rearrange("(k p) t -> p k t", p=128)),
    ])
    proj_tok_res_prefetch(nc, wv, wres, s_wv)

    emit_ln(nc, C, xT, xlnT, mrcol, gate=None)
    emit_proj_feat(nc, C, wq, NK, xlnT, q2t)
    emit_proj_feat(nc, C, wk, NK, xlnT, k2t)
    s_v2st = DSem(nc, "v2st")
    emit_proj_tok_res(nc, C, wv, xlnT,
                      evac_stage_consume_tc(vstg, s_v2st), wres, s_wv,
                      dma_out=(vstg, v2t, s_v2st))
    nc.finalize()
    return nc


def build_flash_attn():
    nc = bass.Bass("TRN2", target_bir_lowering=False)
    C = Consts(nc)
    qh = nc.dram_tensor("qh", [HPC * 128, S], BF16, kind="ExternalInput").ap()
    kh = nc.dram_tensor("kh", [HPC * 128, S], BF16, kind="ExternalInput").ap()
    vh = nc.dram_tensor("vh", [S, HPC * 128], BF16, kind="ExternalInput").ap()
    msk = nc.dram_tensor("msk", [128, 2048], BF16, kind="ExternalInput").ap()
    yt = nc.dram_tensor("yt", [HPC * 128, S], BF16, kind="ExternalOutput").ap()
    wo_f = nc.dram_tensor("wo_f", [D // W, D], F32, kind="ExternalInput").ap()
    wo_b = nc.dram_tensor("wo_b", [D // W, D], BF16, kind="ExternalOutput").ap()

    qhT = nc.alloc_sbuf_tensor("qhT", [128, HPC * S], BF16)
    khT = nc.alloc_sbuf_tensor("khT", [128, HPC * S], BF16)
    vt = nc.alloc_sbuf_tensor("vt", [128, (S // 128) * 256], BF16)
    masks = nc.alloc_sbuf_tensor("masks", [128, 2048], BF16)

    C.emit(nc)
    side_section = make_side_cast(nc, [(wo_f, wo_b)])
    emit_loads(nc, [
        (qhT[:, :].rearrange("p (hl t) -> p hl t", t=S),
         qh[:, :].rearrange("(hl p) t -> p hl t", p=128)),
        (khT[:, :].rearrange("p (hl t) -> p hl t", t=S),
         kh[:, :].rearrange("(hl p) t -> p hl t", p=128)),
        (vt[:, :].rearrange("p (j c) -> p j c", c=256),
         vh[:, :].rearrange("(j p) c -> p j c", p=128)),
        (masks[:, :], msk[:, :]),
    ])

    def y_out(e, hl, qb, qi, src, s_Y):
        s_Y.inc(qi, e.dma_start(
            out=yt[128 * hl:128 * hl + 128, 512 * qb:512 * qb + 512],
            in_=src))

    emit_flash2(nc, C, qhT, khT, S, vt, 256, 128, (masks[:, 0:1024], masks[:, 1024:2048]),
                1.0, BF16, y_out, side_section=side_section)
    nc.finalize()
    return nc


def build_flash_ffn(final=False):
    nc = bass.Bass("TRN2", target_bir_lowering=False)
    C = Consts(nc)
    qh = nc.dram_tensor("qh", [HPC * 128, S], BF16, kind="ExternalInput").ap()
    khx = nc.dram_tensor("khx", [128, S], BF16, kind="ExternalInput").ap()
    vv = nc.dram_tensor("vv", [S, 128], BF16, kind="ExternalInput").ap()
    msk = nc.dram_tensor("msk", [128, 2048], BF16, kind="ExternalInput").ap()
    side = []
    if not final:
        R8 = D // W
        wq_f = nc.dram_tensor("wq_f", [R8, D], F32, kind="ExternalInput").ap()
        wk_f = nc.dram_tensor("wk_f", [R8, D], F32, kind="ExternalInput").ap()
        wv_f = nc.dram_tensor("wv_f", [R8, D], F32, kind="ExternalInput").ap()
        wkv_f = nc.dram_tensor("wkv_f", [R8, 2 * D], F32, kind="ExternalInput").ap()
        wq1_f = nc.dram_tensor("wq1_f", [R8, D], F32, kind="ExternalInput").ap()
        wq_b = nc.dram_tensor("wq_b", [R8, D], BF16, kind="ExternalOutput").ap()
        wk_b = nc.dram_tensor("wk_b", [R8, D], BF16, kind="ExternalOutput").ap()
        wv_b = nc.dram_tensor("wv_b", [R8, D], BF16, kind="ExternalOutput").ap()
        wkv_b = nc.dram_tensor("wkv_b", [R8, 2 * D], BF16, kind="ExternalOutput").ap()
        wq1_b = nc.dram_tensor("wq1_b", [R8, D], BF16, kind="ExternalOutput").ap()
        side = [(wq_f, wq_b), (wk_f, wk_b), (wv_f, wv_b),
                (wkv_f[:, 0:D], wkv_b[:, 0:D]),
                (wkv_f[:, D:2 * D], wkv_b[:, D:2 * D]),
                (wq1_f, wq1_b)]
    x2h = nc.dram_tensor("x2h", [HPC * 128, S], F32, kind="ExternalInput").ap()
    x3h = nc.dram_tensor("x3h", [HPC * 128, S], F32, kind="ExternalOutput").ap()

    qhT = nc.alloc_sbuf_tensor("qhT", [128, HPC * S], BF16)
    khT = nc.alloc_sbuf_tensor("khT", [128, S], BF16)
    vt = nc.alloc_sbuf_tensor("vt", [128, S], BF16)
    masks = nc.alloc_sbuf_tensor("masks", [128, 2048], BF16)

    C.emit(nc)
    side_section = make_side_cast(nc, side) if side else None
    x2sb = nc.alloc_sbuf_tensor("x2sb", [128, HPC * S], F32)
    loads = [
        (khT[:, :], khx[:, :]),
        (qhT[:, :].rearrange("p (hl t) -> p hl t", t=S),
         qh[:, :].rearrange("(hl p) t -> p hl t", p=128)),
        (vt[:, :].rearrange("p (j c) -> p j c", c=128),
         vv[:, :].rearrange("(j p) c -> p j c", p=128)),
        (masks[:, :], msk[:, :]),
        (x2sb[:, :].rearrange("p (hl t) -> p hl t", t=S),
         x2h[:, :].rearrange("(hl p) t -> p hl t", p=128)),
    ]
    emit_loads(nc, loads)

    def y_out(e, hl, qb, qi, src, s_Y):
        s_Y.inc(qi, e.dma_start(
            out=x3h[128 * hl:128 * hl + 128, 512 * qb:512 * qb + 512],
            in_=src))

    emit_flash2(nc, C, qhT, khT, 0, vt, 128, 0,
                (masks[:, 0:1024], masks[:, 1024:2048]), 0.5, F32, y_out,
                add_src=x2sb, side_section=side_section)

    nc.finalize()
    return nc



def _causal_masks():
    m = np.zeros((128, 2048), np.float32)
    for half, moff in ((0, 0), (1, 1), (2, 2), (3, 3)):
        j = np.arange(128)[:, None]
        i = np.arange(512)[None, :]
        m[:, 512 * half:512 * half + 512] = (i >= j + 128 * moff)
    return m.astype(BF)


def _run(nc, in_maps):
    res = run_bass_kernel_spmd(nc, in_maps, core_ids=list(range(W)))
    return res.results


_NC_CACHE = {}


def _get(name, builder, *a, **k):
    if name not in _NC_CACHE:
        _NC_CACHE[name] = builder(*a, **k)
    return _NC_CACHE[name]


def kernel(L0_attn_x, L0_attn_Wq, L0_attn_Wk, L0_attn_Wv, L0_attn_Wo,
           L0_ffn_W_gate, L0_ffn_W_qkv):
    x = np.ascontiguousarray(np.asarray(L0_attn_x, np.float32)[0])
    XT = np.ascontiguousarray(x.T)
    Wq = np.asarray(L0_attn_Wq, np.float32)
    Wk = np.asarray(L0_attn_Wk, np.float32)
    Wv = np.asarray(L0_attn_Wv, np.float32)
    Wo = np.asarray(L0_attn_Wo, np.float32)
    Wg = np.asarray(L0_ffn_W_gate, np.float32)
    Wqkv = np.asarray(L0_ffn_W_qkv, np.float32)

    wgT = np.ascontiguousarray(Wg.T)
    wq1T = np.ascontiguousarray(Wqkv[0:D].T)
    wkvT = np.ascontiguousarray(Wqkv[D:3 * D].T)
    wqT = np.ascontiguousarray(Wq.T)
    wkT = np.ascontiguousarray(Wk.T)
    wvT = np.ascontiguousarray(Wv.T)
    woT = np.ascontiguousarray(Wo.T)
    msk = _causal_masks()
    idn = np.eye(128, dtype=np.float32)

    def xslice(A, c):
        return np.ascontiguousarray(A[:, T * c:T * c + T])

    nc1 = _get("prep1", build_prep1, False)
    r1 = _run(nc1, [{"x_t": xslice(XT, c), "wg": wgT, "wkv": wkvT,
                     "wq": wq1T, "idn": idn} for c in range(W)])
    Q1 = np.concatenate([r1[c]["qt"] for c in range(W)], axis=1)
    KT1 = np.concatenate([r1[c]["ksht"] for c in range(W)], axis=1)
    VS1 = np.concatenate([r1[c]["vsht"] for c in range(W)], axis=0)

    nc2 = _get("ffn_flash", build_flash_ffn, False)
    R8 = D // W
    r2 = _run(nc2, [{"qh": np.ascontiguousarray(Q1[256 * c:256 * c + 256]),
                     "khx": KT1, "vv": VS1, "msk": msk,
                     "x2h": np.ascontiguousarray(XT[256 * c:256 * c + 256]),
                     "wq_f": np.ascontiguousarray(wqT[R8 * c:R8 * c + R8]),
                     "wk_f": np.ascontiguousarray(wkT[R8 * c:R8 * c + R8]),
                     "wv_f": np.ascontiguousarray(wvT[R8 * c:R8 * c + R8]),
                     "wkv_f": np.ascontiguousarray(wkvT[R8 * c:R8 * c + R8]),
                     "wq1_f": np.ascontiguousarray(wq1T[R8 * c:R8 * c + R8])}
                    for c in range(W)])
    X1 = np.concatenate([r2[c]["x3h"] for c in range(W)], axis=0)
    wqB = np.concatenate([r2[c]["wq_b"] for c in range(W)], axis=0)
    wkB = np.concatenate([r2[c]["wk_b"] for c in range(W)], axis=0)
    wvB = np.concatenate([r2[c]["wv_b"] for c in range(W)], axis=0)
    wkvB = np.concatenate([r2[c]["wkv_b"] for c in range(W)], axis=0)
    wq1B = np.concatenate([r2[c]["wq1_b"] for c in range(W)], axis=0)

    nc3 = _get("prep2", build_prep2)
    r3 = _run(nc3, [{"x_t": xslice(X1, c),
                     "wq": wqB, "wk": wkB, "wv": wvB} for c in range(W)])
    Q2 = np.concatenate([r3[c]["q2t"] for c in range(W)], axis=1)
    K2 = np.concatenate([r3[c]["k2t"] for c in range(W)], axis=1)
    V2 = np.concatenate([r3[c]["v2t"] for c in range(W)], axis=0)

    nc4 = _get("attn_flash", build_flash_attn)
    r4 = _run(nc4, [{"qh": np.ascontiguousarray(Q2[256 * c:256 * c + 256]),
                     "kh": np.ascontiguousarray(K2[256 * c:256 * c + 256]),
                     "vh": np.ascontiguousarray(V2[:, 256 * c:256 * c + 256]),
                     "msk": msk,
                     "wo_f": np.ascontiguousarray(woT[R8 * c:R8 * c + R8])}
                    for c in range(W)])
    Y2 = np.concatenate([r4[c]["yt"] for c in range(W)], axis=0)
    woB = np.concatenate([r4[c]["wo_b"] for c in range(W)], axis=0)

    nc5 = _get("prep3", build_prep1, True)
    r5 = _run(nc5, [{"x_t": xslice(X1, c), "y2_t": xslice(Y2, c),
                     "wo": woB, "wg": wgT, "wkv": wkvB, "wq": wq1B,
                     "idn": idn}
                    for c in range(W)])
    X2 = np.concatenate([r5[c]["x2_t"] for c in range(W)], axis=1)
    Q3 = np.concatenate([r5[c]["qt"] for c in range(W)], axis=1)
    KT3 = np.concatenate([r5[c]["ksht"] for c in range(W)], axis=1)
    VS3 = np.concatenate([r5[c]["vsht"] for c in range(W)], axis=0)

    nc6 = _get("ffn_flash_fin", build_flash_ffn, True)
    r6 = _run(nc6, [{"qh": np.ascontiguousarray(Q3[256 * c:256 * c + 256]),
                     "khx": KT3, "vv": VS3, "msk": msk,
                     "x2h": np.ascontiguousarray(X2[256 * c:256 * c + 256])}
                    for c in range(W)])
    X3 = np.concatenate([r6[c]["x3h"] for c in range(W)], axis=0)

    return np.ascontiguousarray(X3.T).reshape(1, S, D)
